# version 07
# baseline (speedup 1.0000x reference)
import contextlib
import numpy as np
import ml_dtypes
import concourse.bass as bass
import concourse.mybir as mybir
from concourse.bass_utils import run_bass_kernel_spmd

F32 = mybir.dt.float32
BF16 = mybir.dt.bfloat16
AL = mybir.AluOpType
AF = mybir.ActivationFunctionType
AX = mybir.AxisListType

NCORE = 8
TOK = 1024
TOKE = TOK + 3
H = 2048
NKH, NVH, DK, DV, KS = 8, 16, 64, 128, 4
KD, VD = NKH * DK, NVH * DV
CD = 2 * KD + VD
IPD = 5376
C = 128
NCH = TOK // C
E, I_MOE, SI = 32, 768, 4096
SCALE = DK ** -0.5
BF = ml_dtypes.bfloat16

DBG_TAPS = []
DBG_STOP_AFTER = None
DBG_P1_OTILES = None

SEQ_OF = {"pe": "tensor", "act": "scalar", "dve": "vector", "pool": "gpsimd",
          "qs": "sync", "qa": "scalar", "qp": "gpsimd", "cc": "gpsimd"}
DMA_V = ("qs", "qa", "qp", "cc")


class Builder:
    def __init__(self, nc):
        self.nc = nc
        self.ops = []
        self.writers = {}
        self.readers = {}
        self.deps = []
        self.marked = set()
        self.bar = []

    def rec(self, virt, fn, reads=(), writes=()):
        i = len(self.ops)
        dep = set(self.bar)
        for k in reads:
            w = self.writers.get(k)
            if w is not None:
                dep.add(w)
        for k in writes:
            w = self.writers.get(k)
            if w is not None:
                dep.add(w)
            for rd in self.readers.get(k, ()):
                dep.add(rd)
        self.ops.append((SEQ_OF[virt], virt, fn))
        self.deps.append(dep)
        for k in reads:
            self.readers.setdefault(k, []).append(i)
        for k in writes:
            self.writers[k] = i
            self.readers[k] = []
        return i

    def barrier(self):
        last = {}
        for i in range(len(self.ops) - 1, -1, -1):
            v = self.ops[i][1]
            if v not in last:
                last[v] = i
            if len(last) == 8:
                break
        self.bar = list(last.values())

    def finalize(self, sems):
        R = 4
        n = len(self.ops)
        virt_of = [o[1] for o in self.ops]
        real_deps = []
        for i in range(n):
            rd = []
            for d in self.deps[i]:
                if virt_of[d] == virt_of[i] == "pe":
                    continue
                if d == i:
                    continue
                rd.append(d)
                self.marked.add(d)
            real_deps.append(rd)
        slot = {}
        seq_in_slot = {}
        counts = {}
        inc_after = [0] * n
        for i in range(n):
            v = virt_of[i]
            if v in DMA_V:
                k = counts.get(v, 0)
                counts[v] = k + 1
                slot[i] = k % R
                seq_in_slot[i] = k // R
            elif i in self.marked:
                counts[v] = counts.get(v, 0) + 1
            inc_after[i] = counts.get(v, 0)
        per_seq = {}
        for i in range(n):
            per_seq.setdefault(self.ops[i][0], []).append(i)
        nc = self.nc

        def dep_wait(d):
            v = virt_of[d]
            if v in DMA_V:
                return (sems[v][slot[d]], 16 * (seq_in_slot[d] + 1))
            return (sems[v], inc_after[d])

        def make_body(seq):
            idxs = per_seq.get(seq, [])

            def body(eng):
                waited = {}
                for i in idxs:
                    wl = [dep_wait(d) for d in real_deps[i]]
                    v = virt_of[i]
                    if v in DMA_V and seq_in_slot[i] > 0:
                        wl.append((sems[v][slot[i]], 16 * seq_in_slot[i]))
                    for s, val in wl:
                        key = id(s)
                        if val > waited.get(key, -1):
                            eng.wait_ge(s, val)
                            waited[key] = val
                    inst = self.ops[i][2](eng)
                    if v in DMA_V:
                        inst.then_inc(sems[v][slot[i]], 16)
                    elif i in self.marked:
                        inst.then_inc(sems[v], 1)
            return body

        with nc.Block() as block:
            block.sync(make_body("sync"))
            block.tensor(make_body("tensor"))
            block.scalar(make_body("scalar"))
            block.vector(make_body("vector"))
            block.gpsimd(make_body("gpsimd"))


def build_program(split=None):
    nc = bass.Bass("TRN2", target_bir_lowering=False)
    B = Builder(nc)
    gst = contextlib.ExitStack()
    nc._gst = gst
    sb = {}

    def S(stk, name, shape, dtype=F32, side=None):
        kw = {"side": side} if side else {}
        sb[name] = stk.enter_context(
            nc.sbuf_tensor('s_' + name, list(shape), dtype, **kw))
        return sb[name]

    din = {}

    def DI(name, shape, dtype=F32):
        din[name] = nc.dram_tensor(name, list(shape), dtype, kind="ExternalInput")
        return din[name]
    nc._din = din

    if split != "B":
        xT = DI("xT", [H, TOKE])
        wipT = DI("wipT", [H, IPD])
        convw_d = DI("convw", [128, 24 * KS])
        dtb_d = DI("dtb", [16, 1])
        alog_d = DI("alog", [16, 1])
    if split != "A":
        xtm = DI("xtm", [TOK, H])
        normw_d = DI("normw", [1, H])
        wopT_d = DI("wopT", [VD, H])
        routerw_d = DI("routerw", [H, E])
        wg_d = DI("wg", [E, 6, 128, 16, 128])
        wu_d = DI("wu", [E, 6, 128, 16, 128])
        wd_d = DI("wd", [E, I_MOE, H])
        wsg_d = DI("wsg", [SI // 128, 128, 16, 128])
        wsu_d = DI("wsu", [SI // 128, 128, 16, 128])
        wsd_d = DI("wsd", [SI, H])
        seg_in_d = DI("segall_in", [64, NCORE * NVH * 192])
        pt_in = DI("pt_in", [64, NVH * NCH * 64], BF16)
        g_in = DI("g_in", [64, NVH * NCH * 128], BF16)
        lo_in = DI("lo_in", [64, NVH * NCH * 128], BF16)
        otm_in = DI("otm_in", [128, NCH * VD], BF16)
        tc_in = DI("tc_in", [128, NCH * 64])
        z_in = DI("z_in", [TOK, VD], BF16)
    identf_d = DI("identf", [128, 128])
    identb_d = DI("identb", [128, 128], BF16)
    ones1f_d = DI("ones1f", [1, 128])
    ones2b_d = DI("ones2b", [128, 2], BF16)
    blk2f_d = DI("blk2f", [2, 128])
    maskS_d = DI("maskS", [128, 128])
    maskI_d = DI("maskI", [128, 128])
    segmask_d = DI("segmask", [64, NCORE])
    cvals_d = DI("cvals", [128, 3])

    out_d = nc.dram_tensor("out", [TOK, H], F32, kind="ExternalOutput")
    if split == "A":
        seg_o = nc.dram_tensor("segout", [64, NVH * 192], F32, kind="ExternalOutput")
        pt_o = nc.dram_tensor("pt_o", [64, NVH * NCH * 64], BF16, kind="ExternalOutput")
        g_o = nc.dram_tensor("g_o", [64, NVH * NCH * 128], BF16, kind="ExternalOutput")
        lo_o = nc.dram_tensor("lo_o", [64, NVH * NCH * 128], BF16, kind="ExternalOutput")
        otm_o = nc.dram_tensor("otm_o", [128, NCH * VD], BF16, kind="ExternalOutput")
        tc_o = nc.dram_tensor("tc_o", [128, NCH * 64], F32, kind="ExternalOutput")
    cc_in = nc.dram_tensor("cc_in", [NVH * 64, 192], F32, kind="Internal")
    cc_out = nc.dram_tensor("cc_out", [NCORE * NVH * 64, 192], F32, kind="Internal", addr_space="Shared")
    x1_sp = nc.dram_tensor("x1spill", [TOK, H], BF16, kind="Internal")
    x1pre = nc.dram_tensor("x1pre", [TOK, H], F32, kind="Internal")
    z_sp = nc.dram_tensor("zspill", [TOK, VD], BF16,
                      kind="ExternalOutput" if split == "A" else "Internal")
    dbg_outs = {}
    for (oname, sname, sl) in DBG_TAPS:
        dbg_outs[oname] = nc.dram_tensor(
            oname, [sl[1] - sl[0], sl[3] - sl[2]], F32, kind="ExternalOutput")

    ps = {}
    for nm in ("b0", "b1", "b2", "b3", "b5", "b7"):
        ps[nm] = gst.enter_context(nc.psum_tensor(nm, [128, 512], F32))
    ps["b4"] = gst.enter_context(nc.psum_tensor("b4", [128, 1024], BF16))
    ps["b6"] = gst.enter_context(nc.psum_tensor("b6", [128, 1024], BF16))

    sems = {}
    for v in ("pe", "act", "dve", "pool"):
        sems[v] = gst.enter_context(nc.semaphore(f"s_{v}"))
    for v in ("qs", "qa", "qp", "cc"):
        sems[v] = [gst.enter_context(nc.semaphore(f"s_{v}{j}")) for j in range(4)]

    r = B.rec

    def dma(virt, dst_ap, src_ap, reads, writes):
        r(virt, lambda e, d=dst_ap, s=src_ap: e.dma_start(out=d, in_=s),
          reads, writes)

    for nm, shp, dt in [("identf", [128, 128], F32), ("identb", [128, 128], BF16),
                        ("ones1f", [1, 128], F32), ("ones2b", [128, 2], BF16),
                        ("blk2f", [2, 128], F32), ("maskS", [128, 128], F32),
                        ("maskI", [128, 128], F32), ("segmask", [64, NCORE], F32),
                        ("convw", [128, 24 * KS], F32), ("dtb", [16, 1], F32),
                        ("alog", [16, 1], F32), ("nexpA", [16, 1], F32),
                        ("tcols", [128, NCH * 64], F32),
                        ("o_tm", [128, NCH * VD], BF16),
                        ("combine", [128, NCH * E], F32),
                        ("red1", [128, 1], F32), ("red1b", [128, 1], F32),
                        ("red1c", [128, 1], F32), ("cvals", [128, 3], F32)]:
        S(gst, nm, shp, dt)

    p0 = [("identf", identf_d), ("identb", identb_d), ("ones1f", ones1f_d),
          ("ones2b", ones2b_d), ("blk2f", blk2f_d), ("maskS", maskS_d),
          ("maskI", maskI_d), ("segmask", segmask_d), ("cvals", cvals_d)]
    if split != "B":
        p0 += [("convw", convw_d), ("dtb", dtb_d), ("alog", alog_d)]
    for nm, dr in p0:
        dma("qs", sb[nm][:, :], dr.ap(), (), ((nm,),))
    nc.const_aps.register(F32, 1.0, sb["cvals"][:, 0:1])
    nc.const_aps.register(F32, 1e-6, sb["cvals"][:, 1:2])
    nc.const_aps.register(F32, 1e-5, sb["cvals"][:, 2:3])
    if split != "B":
        r("act", lambda e: e.activation(sb["nexpA"][:16, :], sb["alog"][:16, :],
                                        AF.Exp),
          (("alog",),), (("nexpA",),))
        r("dve", lambda e: e.tensor_scalar_mul(sb["nexpA"][:16, :],
                                               sb["nexpA"][:16, :], -1.0),
          (("nexpA",),), (("nexpA",),))

    if DBG_STOP_AFTER == "P0":
        _finish(B, nc, sems, sb, dbg_outs, out_d, early=True)
        return nc

    if split != "B":
        stAB = contextlib.ExitStack()
        S(stAB, "q_bf", [128, 4 * TOK], BF16, side="right")
        S(stAB, "k_bf", [128, 4 * TOK], BF16, side="right")
        S(stAB, "v_bf", [128, 16 * TOK], BF16, side="right")

        stA = contextlib.ExitStack()
        S(stA, "xfm", [128, 16 * 1028], BF16)
        S(stA, "wipb", [128, 2 * 16 * 512], BF16)
        S(stA, "stage", [128, 2 * TOKE])
        S(stA, "qkvt", [128, TOK])
        S(stA, "sqb", [128, TOK], BF16)
        S(stA, "rr", [2, TOK])
        S(stA, "gt", [16, TOK])
        S(stA, "ct16", [16, TOK])
        S(stA, "bt16", [16, TOK])
        S(stA, "blt16", [16, TOK])
        S(stA, "nbt16", [16, TOK])
        S(stA, "lamt", [16, TOK])
        S(stA, "stk64", [64, TOK], side="right")
        S(stA, "spt", [16, 2 * TOK])
        S(stA, "ztr", [128, 128], BF16)

        for hc in range(16):
            dma("qp", sb["xfm"][:, hc * 1028: hc * 1028 + TOKE],
                xT.ap()[hc * 128:(hc + 1) * 128, :], (), (("xfm", hc),))

        ogs = [(i * 4, 4) for i in range(10)] + [(40, 2)]
        if DBG_P1_OTILES is not None:
            no = DBG_P1_OTILES
            ogs = [(o0, min(olen, no - o0)) for (o0, olen) in ogs if o0 < no]
        for og, (o0, olen) in enumerate(ogs):
            sl = og % 2
            ow = olen * 128
            for hc in range(16):
                dma("qp", sb["wipb"][:, (sl * 16 + hc) * 512:(sl * 16 + hc) * 512 + ow],
                    wipT.ap()[hc * 128:(hc + 1) * 128, o0 * 128: o0 * 128 + ow],
                    (("wipb", sl, hc),), (("wipb", sl, hc),))
            for otl in range(olen):
                ot = o0 + otl
                ssl = ot % 2
                for tch, (t0, tl) in enumerate([(0, 512), (512, 512), (1024, 3)]):
                    bank = "b0" if (ot * 3 + tch) % 2 == 0 else "b1"
                    for hc in range(16):
                        r("pe", lambda e, hc=hc, sl=sl, otl=otl, t0=t0, tl=tl, bank=bank:
                          e.matmul(ps[bank][:, :tl],
                                   sb["wipb"][:, (sl * 16 + hc) * 512 + otl * 128:
                                              (sl * 16 + hc) * 512 + (otl + 1) * 128],
                                   sb["xfm"][:, hc * 1028 + t0: hc * 1028 + t0 + tl],
                                   start=(hc == 0), stop=(hc == 15)),
                          (("wipb", sl, hc), ("xfm", hc)), ((bank,),))
                    r("act", lambda e, ssl=ssl, t0=t0, tl=tl, bank=bank: e.copy(
                        sb["stage"][:, ssl * TOKE + t0: ssl * TOKE + t0 + tl],
                        ps[bank][:, :tl]),
                      ((bank,),), (("stage", ssl, tch),))
                stg = sb["stage"][:, ssl * TOKE:(ssl + 1) * TOKE]
                srd = (("stage", ssl, 0), ("stage", ssl, 1), ("stage", ssl, 2))
                if ot < 24:
                    qsl = 0
                    qv = sb["qkvt"][:, 0:TOK]
                    r("dve", lambda e, stg=stg, qv=qv, ot=ot: e.tensor_scalar(
                        qv, stg[:, 0:TOK], sb["convw"][:, ot * 4:ot * 4 + 1], None, AL.mult),
                      srd + (("convw",),), (("qkvt", qsl),))
                    for k in (1, 2, 3):
                        r("dve", lambda e, stg=stg, qv=qv, ot=ot, k=k:
                          e.scalar_tensor_tensor(qv, stg[:, k:k + TOK],
                                                 sb["convw"][:, ot * 4 + k:ot * 4 + k + 1],
                                                 qv, AL.mult, AL.add),
                          srd + (("qkvt", qsl), ("convw",)), (("qkvt", qsl),))
                    r("act", lambda e, qv=qv: e.activation(qv, qv, AF.Silu),
                      (("qkvt", qsl),), (("qkvt", qsl),))
                    if ot < 8:
                        isq = ot < 4
                        dst = sb["q_bf"] if isq else sb["k_bf"]
                        dn = "q_bf" if isq else "k_bf"
                        ti = ot if isq else ot - 4
                        r("dve", lambda e, qv=qv: e.tensor_mul(sb["sqb"][:, :], qv, qv),
                          (("qkvt", qsl),), (("sqb",),))
                        for hf, bank in ((0, "b2"), (1, "b3")):
                            r("pe", lambda e, hf=hf, bank=bank: e.matmul(
                                ps[bank][:2, :512], sb["ones2b"][:, :2],
                                sb["sqb"][:, hf * 512:(hf + 1) * 512],
                                start=True, stop=True),
                              (("sqb",), ("ones2b",)), ((bank,),))
                            r("act", lambda e, hf=hf, bank=bank: e.copy(
                                sb["rr"][:2, hf * 512:(hf + 1) * 512], ps[bank][:2, :512]),
                              ((bank,),), (("rr", hf),))
                        rrv = sb["rr"][:2, 0:TOK]
                        r("dve", lambda e, rrv=rrv: e.tensor_scalar_max(rrv, rrv, 1e-24),
                          (("rr", 0), ("rr", 1)), (("rr", 0), ("rr", 1)))
                        r("act", lambda e, rrv=rrv: e.activation(rrv, rrv, AF.Ln),
                          (("rr", 0), ("rr", 1)), (("rr", 0), ("rr", 1)))
                        r("act", lambda e, rrv=rrv: e.activation(rrv, rrv, AF.Exp, scale=-0.5),
                          (("rr", 0), ("rr", 1)), (("rr", 0), ("rr", 1)))
                        if isq:
                            r("dve", lambda e, rrv=rrv: e.tensor_scalar_mul(rrv, rrv, SCALE),
                              (("rr", 0), ("rr", 1)), (("rr", 0), ("rr", 1)))
                        for hf, bank in ((0, "b2"), (1, "b3")):
                            r("pe", lambda e, hf=hf, bank=bank: e.matmul(
                                ps[bank][:, :512], sb["blk2f"][:2, :],
                                sb["rr"][:2, hf * 512:(hf + 1) * 512],
                                start=True, stop=True),
                              (("rr", 0), ("rr", 1), ("blk2f",)), ((bank,),))
                            r("dve", lambda e, qv=qv, dst=dst, ti=ti, hf=hf, bank=bank:
                              e.tensor_mul(
                                  dst[:, ti * TOK + hf * 512: ti * TOK + (hf + 1) * 512],
                                  qv[:, hf * 512:(hf + 1) * 512], ps[bank][:, :512]),
                              ((bank,), ("qkvt", qsl)), ((dn, ti),))
                    else:
                        vt = ot - 8
                        r("pool", lambda e, qv=qv, vt=vt: e.tensor_copy(
                            sb["v_bf"][:, vt * TOK:(vt + 1) * TOK], qv),
                          (("qkvt", qsl),), (("v_bf", vt),))
                elif ot < 40:
                    zt = ot - 24
                    for ch in range(NCH):
                        r("pe", lambda e, stg=stg, ch=ch: e.transpose(
                            ps["b2"][:, :128], stg[:, 3 + ch * 128: 3 + (ch + 1) * 128],
                            sb["identf"][:, :]),
                          srd + (("identf",),), (("b2",),))
                        r("act", lambda e: e.copy(sb["ztr"][:, :], ps["b2"][:, :128]),
                          (("b2",),), (("ztr",),))
                        dma("qs", z_sp.ap()[ch * 128:(ch + 1) * 128,
                                            zt * 128:(zt + 1) * 128],
                            sb["ztr"][:, :], (("ztr",),), (("zsp", ch, zt),))
                elif ot == 40:
                    r("act", lambda e, stg=stg: e.activation(
                        sb["bt16"][:16, :], stg[:16, 3:TOKE], AF.Sigmoid),
                      srd, (("stk", "beta"),))
                else:
                    sp = sb["spt"]
                    r("dve", lambda e, stg=stg, sp=sp: e.tensor_scalar(
                        sp[:16, 0:TOK], stg[:16, 3:TOKE], sb["dtb"][:16, :], None, AL.add),
                      srd + (("dtb",),), (("spt", 0),))
                    r("dve", lambda e, sp=sp: e.tensor_scalar_mul(
                        sp[:16, TOK:2 * TOK], sp[:16, 0:TOK], -1.0),
                      (("spt", 0),), (("spt", 1),))
                    r("dve", lambda e, sp=sp: e.tensor_max(
                        sp[:16, TOK:2 * TOK], sp[:16, 0:TOK], sp[:16, TOK:2 * TOK]),
                      (("spt", 0), ("spt", 1)), (("spt", 1),))
                    r("act", lambda e, sp=sp: e.activation(
                        sp[:16, TOK:2 * TOK], sp[:16, TOK:2 * TOK], AF.Exp, scale=-1.0),
                      (("spt", 1),), (("spt", 1),))
                    r("act", lambda e, sp=sp: e.activation(
                        sp[:16, TOK:2 * TOK], sp[:16, TOK:2 * TOK], AF.Ln, bias=1.0),
                      (("spt", 1), ("cvals",)), (("spt", 1),))
                    r("dve", lambda e, sp=sp: e.tensor_scalar_max(
                        sb["gt"][:16, :], sp[:16, 0:TOK], 0.0),
                      (("spt", 0),), (("gt",),))
                    r("dve", lambda e, sp=sp: e.tensor_add(
                        sb["gt"][:16, :], sb["gt"][:16, :], sp[:16, TOK:2 * TOK]),
                      (("spt", 1), ("gt",)), (("gt",),))
                    r("dve", lambda e: e.tensor_scalar(
                        sb["gt"][:16, :], sb["gt"][:16, :], sb["nexpA"][:16, :], None, AL.mult),
                      (("gt",), ("nexpA",)), (("gt",),))

        if DBG_P1_OTILES is not None:
            _finish(B, nc, sems, sb, dbg_outs, out_d, early=True)
            return nc
        r("dve", lambda e: e.tensor_copy(sb["spt"][:16, 0:TOK], sb["gt"][:16, :]),
          (("gt",), ("spt", 0)), (("spt", 0),))
        cur, oth = 0, 1
        for s in (1, 2, 4, 8, 16, 32, 64):
            for ch in range(NCH):
                r("dve", lambda e, ch=ch, s=s, cur=cur, oth=oth: e.tensor_add(
                    sb["spt"][:16, oth * TOK + ch * C + s: oth * TOK + (ch + 1) * C],
                    sb["spt"][:16, cur * TOK + ch * C + s: cur * TOK + (ch + 1) * C],
                    sb["spt"][:16, cur * TOK + ch * C: cur * TOK + (ch + 1) * C - s]),
                  (("spt", cur),), (("spt", oth),))
                r("dve", lambda e, ch=ch, s=s, cur=cur, oth=oth: e.tensor_copy(
                    sb["spt"][:16, oth * TOK + ch * C: oth * TOK + ch * C + s],
                    sb["spt"][:16, cur * TOK + ch * C: cur * TOK + ch * C + s]),
                  (("spt", cur),), (("spt", oth),))
            cur, oth = oth, cur
        r("dve", lambda e, cur=cur: e.tensor_copy(sb["ct16"][:16, :],
                                                  sb["spt"][:16, cur * TOK:(cur + 1) * TOK]),
          (("spt", cur),), (("stk", "c"),))
        r("act", lambda e: e.activation(sb["lamt"][:16, :], sb["ct16"][:16, :], AF.Exp),
          (("stk", "c"),), (("lamt",),))
        r("dve", lambda e: e.tensor_mul(sb["blt16"][:16, :], sb["bt16"][:16, :],
                                        sb["lamt"][:16, :]),
          (("stk", "beta"), ("lamt",)), (("stk", "bl"),))
        r("dve", lambda e: e.tensor_scalar_mul(sb["nbt16"][:16, :], sb["bt16"][:16, :], -1.0),
          (("stk", "beta"),), (("stk", "nb"),))
        for gi, (nm, key) in enumerate([("ct16", ("stk", "c")),
                                        ("bt16", ("stk", "beta")),
                                        ("blt16", ("stk", "bl")),
                                        ("nbt16", ("stk", "nb"))]):
            dma("qs", sb["stk64"][gi * 16:(gi + 1) * 16, :], sb[nm][:16, :],
                (key,), (("stk64",),))
        for ch in range(NCH):
            r("pe", lambda e, ch=ch: e.transpose(
                ps["b2"][:, :64], sb["stk64"][:64, ch * C:(ch + 1) * C],
                sb["identf"][:64, :64]),
              (("stk64",), ("identf",)), (("b2",),))
            r("act", lambda e, ch=ch: e.copy(
                sb["tcols"][:, ch * 64:(ch + 1) * 64], ps["b2"][:, :64]),
              (("b2",),), (("tcols", ch),))

        B.barrier()
        stA.close()
        if DBG_STOP_AFTER == "P1":
            _finish(B, nc, sems, sb, dbg_outs, out_d, early=True)
            return nc

    stBC = contextlib.ExitStack()
    S(stBC, "PT", [64, NVH * NCH * 64], BF16)
    S(stBC, "G", [64, NVH * NCH * 128], BF16)
    S(stBC, "LO", [64, NVH * NCH * 128], BF16)

    if split == "B":
        dma("qs", sb["PT"][:, :], pt_in.ap(),
            (), tuple(("PT", h, ch) for h in range(NVH) for ch in range(NCH)))
        dma("qs", sb["G"][:, :], g_in.ap(),
            (), tuple(("G", h, ch) for h in range(NVH) for ch in range(NCH)))
        dma("qs", sb["LO"][:, :], lo_in.ap(),
            (), tuple(("LO", h, ch) for h in range(NVH) for ch in range(NCH)))
        dma("qs", sb["o_tm"][:, :], otm_in.ap(),
            (), tuple(("o_tm", ch, h) for ch in range(NCH) for h in range(NVH)))
        dma("qs", sb["tcols"][:, :], tc_in.ap(),
            (), tuple(("tcols", ch) for ch in range(NCH)))

    if split != "B":
        stB = contextlib.ExitStack()
        for nm, shp, dt in [("crow", [1, 128], F32), ("cCcol", [128, 1], F32),
                            ("cdp", [128, 128], F32), ("cdm", [128, 128], F32),
                            ("Dst", [128, 128], F32), ("Din", [128, 128], F32),
                            ("Bb", [128, 128], BF16), ("Mb", [128, 128], BF16),
                            ("Btb", [128, 128], BF16), ("Mtb", [128, 128], BF16),
                            ("Ktm", [128, 64], BF16), ("Vtm", [128, 128], BF16),
                            ("Xf", [128, 192], F32), ("Xb", [128, 192], BF16),
                            ("Pa", [128, 2 * 128], BF16), ("Pta", [128, 2 * 128], BF16),
                            ("kplog", [128, 1], F32), ("kpcol", [128, 1], F32),
                            ("lamc", [128, 1], F32), ("Kp", [128, 64], BF16),
                            ("lamrep", [64, 128], F32), ("t1", [64, 128], F32),
                            ("qb0", [64, 128], BF16)]:
            S(stB, nm, shp, dt)

        def qslice(h):
            kh = h // 2
            return kh // 2, 64 * (kh % 2)

        for ch in range(NCH):
            for h in range(NVH):
                ccol = sb["tcols"][:, ch * 64 + h: ch * 64 + h + 1]
                bcol = sb["tcols"][:, ch * 64 + 16 + h: ch * 64 + 16 + h + 1]
                blcol = sb["tcols"][:, ch * 64 + 32 + h: ch * 64 + 32 + h + 1]
                nbcol = sb["tcols"][:, ch * 64 + 48 + h: ch * 64 + 48 + h + 1]
                ti, pb = qslice(h)
                qv = sb["q_bf"][pb:pb + 64, ti * TOK + ch * C: ti * TOK + (ch + 1) * C]
                kv = sb["k_bf"][pb:pb + 64, ti * TOK + ch * C: ti * TOK + (ch + 1) * C]
                tc = ("tcols", ch)
                r("pe", lambda e, ccol=ccol: e.transpose(ps["b2"][:1, :128], ccol,
                                                         sb["identf"][:, :]),
                  (tc, ("identf",)), (("b2",),))
                r("dve", lambda e: e.tensor_copy(sb["crow"][:1, :], ps["b2"][:1, :128]),
                  (("b2",),), (("crow",),))
                r("pe", lambda e: e.matmul(ps["b2"][:, :1], sb["ones1f"][:1, :128],
                                           sb["crow"][:1, 127:128], start=True, stop=True),
                  (("crow",), ("ones1f",)), (("b2",),))
                r("dve", lambda e: e.tensor_copy(sb["cCcol"][:, :], ps["b2"][:, :1]),
                  (("b2",),), (("cCcol",),))
                r("pe", lambda e: e.matmul(ps["b2"][:, :128], sb["ones1f"][:1, :128],
                                           sb["crow"][:1, :128], start=True, stop=True),
                  (("crow",), ("ones1f",)), (("b2",),))
                r("dve", lambda e, ccol=ccol: e.tensor_scalar(
                    sb["cdp"][:, :], ps["b2"][:, :128], ccol, None, AL.subtract),
                  (("b2",), tc), (("cdp",),))
                r("pe", lambda e, kv=kv: e.matmul(ps["b3"][:, :128], kv, kv,
                                                  start=True, stop=True),
                  (("k_bf", ti),), (("b3",),))
                r("pe", lambda e, qv=qv, kv=kv: e.matmul(ps["b5"][:, :128], qv, kv,
                                                         start=True, stop=True),
                  (("q_bf", ti), ("k_bf", ti)), (("b5",),))
                r("dve", lambda e: e.tensor_add(sb["cdm"][:, :], sb["cdp"][:, :],
                                                sb["maskS"][:, :]),
                  (("cdp",), ("maskS",)), (("cdm",),))
                r("act", lambda e: e.activation(sb["Dst"][:, :], sb["cdm"][:, :],
                                                AF.Exp, scale=-1.0),
                  (("cdm",),), (("Dst",),))
                r("dve", lambda e, nbcol=nbcol: e.scalar_tensor_tensor(
                    sb["Bb"][:, :], sb["Dst"][:, :], nbcol, ps["b3"][:, :128],
                    AL.mult, AL.mult),
                  (("Dst",), tc, ("b3",)), (("Bb",),))
                r("dve", lambda e: e.tensor_add(sb["cdm"][:, :], sb["cdp"][:, :],
                                                sb["maskI"][:, :]),
                  (("cdp",), ("maskI",)), (("cdm",),))
                r("act", lambda e: e.activation(sb["Din"][:, :], sb["cdm"][:, :],
                                                AF.Exp, scale=-1.0),
                  (("cdm",),), (("Din",),))
                r("dve", lambda e: e.tensor_mul(sb["Mb"][:, :], sb["Din"][:, :],
                                                ps["b5"][:, :128]),
                  (("Din",), ("b5",)), (("Mb",),))
                r("pe", lambda e: e.transpose(ps["b4"][:, :128], sb["Bb"][:, :],
                                              sb["identb"][:, :]),
                  (("Bb",), ("identb",)), (("b4",),))
                r("act", lambda e: e.copy(sb["Btb"][:, :], ps["b4"][:, :128]),
                  (("b4",),), (("Btb",),))
                r("pe", lambda e: e.transpose(ps["b6"][:, :128], sb["Mb"][:, :],
                                              sb["identb"][:, :]),
                  (("Mb",), ("identb",)), (("b6",),))
                r("act", lambda e: e.copy(sb["Mtb"][:, :], ps["b6"][:, :128]),
                  (("b6",),), (("Mtb",),))
                r("pe", lambda e, kv=kv, pb=pb: e.transpose(
                    ps["b4"][:, :64], kv, sb["identb"][pb:pb + 64, pb:pb + 64]),
                  (("k_bf", ti), ("identb",)), (("b4",),))
                r("act", lambda e: e.copy(sb["Ktm"][:, :], ps["b4"][:, :64]),
                  (("b4",),), (("Ktm",),))
                r("pe", lambda e, h=h, ch=ch: e.transpose(
                    ps["b6"][:, :128],
                    sb["v_bf"][:, h * TOK + ch * C: h * TOK + (ch + 1) * C],
                    sb["identb"][:, :]),
                  (("v_bf", h), ("identb",)), (("b6",),))
                r("act", lambda e: e.copy(sb["Vtm"][:, :], ps["b6"][:, :128]),
                  (("b6",),), (("Vtm",),))
                r("dve", lambda e, blcol=blcol: e.tensor_scalar(
                    sb["Xf"][:, 0:64], sb["Ktm"][:, :], blcol, None, AL.mult),
                  (("Ktm",), tc), (("Xf",),))
                r("dve", lambda e, bcol=bcol: e.tensor_scalar(
                    sb["Xf"][:, 64:192], sb["Vtm"][:, :], bcol, None, AL.mult),
                  (("Vtm",), tc), (("Xf",),))
                r("pool", lambda e: e.tensor_copy(sb["Xb"][:, :], sb["Xf"][:, :]),
                  (("Xf",),), (("Xb",),))
                pk, ptk = ("Bb",), ("Btb",)
                pv, ptv = sb["Bb"][:, :], sb["Btb"][:, :]
                for it in range(7):
                    bx = "b0" if it % 2 == 0 else "b1"
                    r("pe", lambda e, ptv=ptv, bx=bx: e.matmul(
                        ps[bx][:, :192], ptv, sb["Xb"][:, :192], start=True, stop=True),
                      (ptk, ("Xb",)), ((bx,),))
                    r("dve", lambda e, bx=bx: e.tensor_add(sb["Xf"][:, :], sb["Xf"][:, :],
                                                           ps[bx][:, :192]),
                      ((bx,), ("Xf",)), (("Xf",),))
                    r("pool", lambda e: e.tensor_copy(sb["Xb"][:, :], sb["Xf"][:, :]),
                      (("Xf",),), (("Xb",),))
                    if it < 6:
                        sl2 = it % 2
                        r("pe", lambda e, ptv=ptv, pv=pv: e.matmul(
                            ps["b7"][:, :128], ptv, pv, start=True, stop=True),
                          (ptk, pk), (("b7",),))
                        r("act", lambda e, sl2=sl2: e.copy(
                            sb["Pa"][:, sl2 * 128:(sl2 + 1) * 128], ps["b7"][:, :128]),
                          (("b7",),), (("Pa", sl2),))
                        r("pe", lambda e, sl2=sl2: e.transpose(
                            ps["b4"][:, :128], sb["Pa"][:, sl2 * 128:(sl2 + 1) * 128],
                            sb["identb"][:, :]),
                          (("Pa", sl2), ("identb",)), (("b4",),))
                        r("act", lambda e, sl2=sl2: e.copy(
                            sb["Pta"][:, sl2 * 128:(sl2 + 1) * 128], ps["b4"][:, :128]),
                          (("b4",),), (("Pta", sl2),))
                        pk, ptk = ("Pa", sl2), ("Pta", sl2)
                        pv = sb["Pa"][:, sl2 * 128:(sl2 + 1) * 128]
                        ptv = sb["Pta"][:, sl2 * 128:(sl2 + 1) * 128]
                r("dve", lambda e, ccol=ccol: e.tensor_sub(sb["kplog"][:, :],
                                                           sb["cCcol"][:, :], ccol),
                  (("cCcol",), tc), (("kplog",),))
                r("act", lambda e: e.activation(sb["kpcol"][:, :], sb["kplog"][:, :], AF.Exp),
                  (("kplog",),), (("kpcol",),))
                r("act", lambda e: e.activation(sb["lamc"][:, :], sb["cCcol"][:, :], AF.Exp),
                  (("cCcol",),), (("lamc",),))
                r("dve", lambda e: e.tensor_scalar(sb["Kp"][:, :], sb["Ktm"][:, :],
                                                   sb["kpcol"][:, :], None, AL.mult),
                  (("Ktm",), ("kpcol",)), (("Kp",),))
                r("pe", lambda e: e.matmul(ps["b3"][:64, :64], sb["Xb"][:, 0:64],
                                           sb["Kp"][:, :], start=True, stop=True),
                  (("Xb",), ("Kp",)), (("b3",),))
                PTd = sb["PT"][:, (h * NCH + ch) * 64:(h * NCH + ch + 1) * 64]
                r("dve", lambda e, PTd=PTd: e.scalar_tensor_tensor(
                    PTd, sb["identf"][:64, :64], sb["lamc"][:64, :], ps["b3"][:64, :64],
                    AL.mult, AL.subtract),
                  (("identf",), ("lamc",), ("b3",)), (("PT", h, ch),))
                r("pe", lambda e: e.matmul(ps["b5"][:64, :128], sb["Kp"][:, :],
                                           sb["Xb"][:, 64:192], start=True, stop=True),
                  (("Kp",), ("Xb",)), (("b5",),))
                Gd = sb["G"][:, (h * NCH + ch) * 128:(h * NCH + ch + 1) * 128]
                r("act", lambda e, Gd=Gd: e.copy(Gd, ps["b5"][:64, :128]),
                  (("b5",),), (("G", h, ch),))
                r("pe", lambda e: e.matmul(ps["b7"][:64, :128], sb["Xb"][:, 0:64],
                                           sb["Mtb"][:, :], start=True, stop=True),
                  (("Xb",), ("Mtb",)), (("b7",),))
                r("pe", lambda e: e.matmul(ps["b2"][:64, :128], sb["ones1f"][:1, :64],
                                           sb["crow"][:1, :128], start=True, stop=True),
                  (("crow",), ("ones1f",)), (("b2",),))
                r("act", lambda e: e.activation(sb["lamrep"][:, :], ps["b2"][:64, :128],
                                                AF.Exp),
                  (("b2",),), (("lamrep",),))
                r("act", lambda e, qv=qv: e.copy(sb["qb0"][:, :], qv),
                  (("q_bf", ti),), (("qb0",),))
                r("dve", lambda e: e.tensor_mul(sb["t1"][:, :], sb["qb0"][:, :],
                                                sb["lamrep"][:, :]),
                  (("qb0",), ("lamrep",)), (("t1",),))
                LOd = sb["LO"][:, (h * NCH + ch) * 128:(h * NCH + ch + 1) * 128]
                r("dve", lambda e, LOd=LOd: e.tensor_sub(LOd, sb["t1"][:, :],
                                                         ps["b7"][:64, :128]),
                  (("t1",), ("b7",)), (("LO", h, ch),))
                bo = "b0" if h % 2 == 0 else "b1"
                r("pe", lambda e, bo=bo: e.matmul(ps[bo][:, :128], sb["Mtb"][:, :],
                                                  sb["Xb"][:, 64:192], start=True, stop=True),
                  (("Mtb",), ("Xb",)), ((bo,),))
                r("dve", lambda e, bo=bo, ch=ch, h=h: e.tensor_copy(
                    sb["o_tm"][:, ch * VD + h * 128: ch * VD + (h + 1) * 128],
                    ps[bo][:, :128]),
                  ((bo,),), (("o_tm", ch, h),))

        B.barrier()
        stB.close()
        stAB.close()
        if DBG_STOP_AFTER == "P3":
            _finish(B, nc, sems, sb, dbg_outs, out_d, early=True)
            return nc

    stC = contextlib.ExitStack()
    S(stC, "segsb", [64, NVH * 192])
    S(stC, "segall", [64, NVH * 192])
    S(stC, "Sg", [64, 128])
    S(stC, "Sgb", [64, 128], BF16)
    S(stC, "Mc", [64, 2 * 64], BF16)
    S(stC, "Sin", [64, NVH * 128])
    S(stC, "Sbf", [64, NVH * 128], BF16)
    S(stC, "ptp", [64, 64], BF16)
    S(stC, "gp", [64, 128])
    S(stC, "tt1", [64, 64])

    if split != "B":
      for h in range(NVH):
          for i in range(NCH):
              PTs = sb["PT"][:, (h * NCH + i) * 64:(h * NCH + i + 1) * 64]
              Gs = sb["G"][:, (h * NCH + i) * 128:(h * NCH + i + 1) * 128]
              if i == 0:
                  r("dve", lambda e, Gs=Gs, h=h: e.tensor_copy(sb["Sg"][:, :], Gs),
                    (("G", h, 0),), (("Sg",),))
              else:
                  r("pe", lambda e, PTs=PTs: e.matmul(
                      ps["b2"][:64, :128], PTs, sb["Sgb"][:, :], start=True, stop=True),
                    (("PT", h, i), ("Sgb",)), (("b2",),))
                  r("dve", lambda e, Gs=Gs, h=h, i=i: e.tensor_add(
                      sb["Sg"][:, :], ps["b2"][:64, :128], Gs),
                    (("b2",), ("G", h, i)), (("Sg",),))
              r("pool", lambda e: e.tensor_copy(sb["Sgb"][:, :], sb["Sg"][:, :]),
                (("Sg",),), (("Sgb",),))
              if i == 0:
                  r("pe", lambda e, PTs=PTs: e.transpose(
                      ps["b4"][:64, :64], PTs, sb["identb"][:64, :64]),
                    (("PT", h, 0), ("identb",)), (("b4",),))
                  r("act", lambda e: e.copy(sb["Mc"][:, 0:64], ps["b4"][:64, :64]),
                    (("b4",),), (("Mc", 0),))
              else:
                  src, dst = (i - 1) % 2, i % 2
                  r("pe", lambda e, PTs=PTs, src=src: e.matmul(
                      ps["b3"][:64, :64], PTs, sb["Mc"][:, src * 64:(src + 1) * 64],
                      start=True, stop=True),
                    (("PT", h, i), ("Mc", src)), (("b3",),))
                  r("act", lambda e, dst=dst: e.copy(
                      sb["Mc"][:, dst * 64:(dst + 1) * 64], ps["b3"][:64, :64]),
                    (("b3",),), (("Mc", dst),))
          r("dve", lambda e, h=h: e.tensor_copy(
              sb["segsb"][:, h * 192:h * 192 + 64], sb["Mc"][:, 64:128]),
            (("Mc", 1),), (("segsb", h),))
          r("dve", lambda e, h=h: e.tensor_copy(
              sb["segsb"][:, h * 192 + 64:h * 192 + 192], sb["Sg"][:, :]),
            (("Sg",),), (("segsb", h),))
    if split == "A":
        dma("qs", seg_o.ap(), sb["segsb"][:, :],
            tuple(("segsb", h) for h in range(NVH)), (("outw", 0),))
        dma("qs", pt_o.ap(), sb["PT"][:, :],
            tuple(("PT", h, ch) for h in range(NVH) for ch in range(NCH)),
            (("outw", 1),))
        dma("qs", g_o.ap(), sb["G"][:, :],
            tuple(("G", h, ch) for h in range(NVH) for ch in range(NCH)),
            (("outw", 2),))
        dma("qs", lo_o.ap(), sb["LO"][:, :],
            tuple(("LO", h, ch) for h in range(NVH) for ch in range(NCH)),
            (("outw", 3),))
        dma("qs", otm_o.ap(), sb["o_tm"][:, :],
            tuple(("o_tm", ch, h) for ch in range(NCH) for h in range(NVH)),
            (("outw", 4),))
        dma("qs", tc_o.ap(), sb["tcols"][:, :],
            tuple(("tcols", ch) for ch in range(NCH)), (("outw", 5),))
        r("qs", lambda e: e.dma_start(out=out_d.ap()[0:128, 0:128],
                                      in_=sb["identf"][:, :]),
          (("identf",),), (("outw", 6),))
        r("pool", lambda e: e.engine_nop(),
          tuple(("outw", i) for i in range(7)) +
          tuple(("zsp", ch, zt) for ch in range(NCH) for zt in range(16)), ())
        B.finalize(sems)
        return nc
    if split is None:
        dma("qp", cc_in.ap().rearrange("(h p) f -> p h f", p=64),
            sb["segsb"][:, :].rearrange("p (h f) -> p h f", f=192),
            tuple(("segsb", h) for h in range(NVH)), (("cc_in",),))
        r("cc", lambda e: e.collective_compute(
            "AllGather", AL.bypass, replica_groups=[list(range(NCORE))],
            ins=[cc_in.ap()], outs=[cc_out.ap()]),
          (("cc_in",),), (("cc_out",),))
    r("dve", lambda e: e.memset(sb["Sin"][:, :], 0.0), (),
      tuple(("Sin", h) for h in range(NVH)))
    r("pool", lambda e: e.memset(sb["Sbf"][:, :], 0.0), (),
      tuple(("Sbf", h) for h in range(NVH)))
    for j in range(NCORE - 1):
        if split == "B":
            dma("qs", sb["segall"][:, :],
                seg_in_d.ap()[:, j * NVH * 192:(j + 1) * NVH * 192],
                (("segall",),), (("segall",),))
        else:
            dma("qp", sb["segall"][:, :].rearrange("p (h f) -> p h f", f=192),
                cc_out.ap()[j * 1024:(j + 1) * 1024, :].rearrange(
                    "(h p) f -> p h f", p=64),
                (("cc_out",), ("segall",)), (("segall",),))
        for h in range(NVH):
            base = h * 192
            mcol = sb["segmask"][:, j:j + 1]
            r("pe", lambda e, base=base: e.transpose(
                ps["b2"][:64, :64], sb["segall"][:, base:base + 64],
                sb["identf"][:64, :64]),
              (("segall",), ("identf",)), (("b2",),))
            r("dve", lambda e: e.tensor_sub(sb["tt1"][:, :], ps["b2"][:64, :64],
                                            sb["identf"][:64, :64]),
              (("b2",), ("identf",)), (("tt1",),))
            r("dve", lambda e, mcol=mcol: e.tensor_scalar(
                sb["tt1"][:, :], sb["tt1"][:, :], mcol, None, AL.mult),
              (("tt1",), ("segmask",)), (("tt1",),))
            r("dve", lambda e: e.tensor_add(sb["ptp"][:, :], sb["tt1"][:, :],
                                            sb["identf"][:64, :64]),
              (("tt1",), ("identf",)), (("ptp",),))
            r("dve", lambda e, base=base, mcol=mcol: e.tensor_scalar(
                sb["gp"][:, :], sb["segall"][:, base + 64:base + 192],
                mcol, None, AL.mult),
              (("segall",), ("segmask",)), (("gp",),))
            r("pe", lambda e, h=h: e.matmul(
                ps["b3"][:64, :128], sb["ptp"][:, :],
                sb["Sbf"][:, h * 128:(h + 1) * 128], start=True, stop=True),
              (("ptp",), ("Sbf", h)), (("b3",),))
            r("dve", lambda e, h=h: e.tensor_add(
                sb["Sin"][:, h * 128:(h + 1) * 128], ps["b3"][:64, :128],
                sb["gp"][:, :]),
              (("b3",), ("gp",)), (("Sin", h),))
            r("pool", lambda e, h=h: e.tensor_copy(
                sb["Sbf"][:, h * 128:(h + 1) * 128],
                sb["Sin"][:, h * 128:(h + 1) * 128]),
              (("Sin", h),), (("Sbf", h),))

    for ch in range(NCH):
        for grp in range(4):
            bo = "b0" if grp % 2 == 0 else "b1"
            bs = "b2" if grp % 2 == 0 else "b3"
            hs = list(range(grp * 4, grp * 4 + 4))
            for h in hs:
                LOs = sb["LO"][:, (h * NCH + ch) * 128:(h * NCH + ch + 1) * 128]
                r("pe", lambda e, LOs=LOs, h=h, bo=bo: e.matmul(
                    ps[bo][:, (h % 4) * 128:(h % 4 + 1) * 128], LOs,
                    sb["Sbf"][:, h * 128:(h + 1) * 128], start=True, stop=True),
                  (("LO", h, ch), ("Sbf", h)), ((bo,),))
            for h in hs:
                PTs = sb["PT"][:, (h * NCH + ch) * 64:(h * NCH + ch + 1) * 64]
                r("pe", lambda e, PTs=PTs, h=h, bs=bs: e.matmul(
                    ps[bs][:64, (h % 4) * 128:(h % 4 + 1) * 128], PTs,
                    sb["Sbf"][:, h * 128:(h + 1) * 128], start=True, stop=True),
                  (("PT", h, ch), ("Sbf", h)), ((bs,),))
            for h in hs:
                r("dve", lambda e, h=h, ch=ch, bo=bo: e.tensor_add(
                    sb["o_tm"][:, ch * VD + h * 128: ch * VD + (h + 1) * 128],
                    sb["o_tm"][:, ch * VD + h * 128: ch * VD + (h + 1) * 128],
                    ps[bo][:, (h % 4) * 128:(h % 4 + 1) * 128]),
                  ((bo,), ("o_tm", ch, h)), (("o_tm", ch, h),))
            for h in hs:
                Gs = sb["G"][:, (h * NCH + ch) * 128:(h * NCH + ch + 1) * 128]
                r("dve", lambda e, h=h, Gs=Gs, bs=bs, ch=ch: e.tensor_add(
                    sb["Sin"][:, h * 128:(h + 1) * 128],
                    ps[bs][:64, (h % 4) * 128:(h % 4 + 1) * 128], Gs),
                  ((bs,), ("G", h, ch)), (("Sin", h),))
            for h in hs:
                r("pool", lambda e, h=h: e.tensor_copy(
                    sb["Sbf"][:, h * 128:(h + 1) * 128],
                    sb["Sin"][:, h * 128:(h + 1) * 128]),
                  (("Sin", h),), (("Sbf", h),))

    B.barrier()
    stC.close()
    stBC.close()
    if DBG_STOP_AFTER == "P5":
        _finish(B, nc, sems, sb, dbg_outs, out_d, early=True)
        return nc

    stDE = contextlib.ExitStack()
    S(stDE, "xlnT", [128, 16 * TOK], BF16, side="right")
    stD = contextlib.ExitStack()
    S(stD, "gatedT", [128, 16 * TOK], BF16)
    S(stD, "wopc", [128, 16 * 512], BF16)
    S(stD, "ztile", [128, VD], BF16)
    S(stD, "ssq", [128, 16])
    S(stD, "rstd16", [128, 16])
    S(stD, "sqh", [128, 128])
    S(stD, "tmp2k", [128, H])
    S(stD, "tmp2kb", [128, H])
    S(stD, "gat_b", [128, H], BF16)
    S(stD, "x1t", [128, 2 * H], BF16)
    S(stD, "xps", [128, 2 * 512])
    S(stD, "ev5", [128, 2 * 512])
    S(stD, "normw", [1, H])
    S(stD, "normw_rep", [128, H])
    S(stD, "xlnTf", [128, 16 * 128])
    S(stD, "eltile", [128, E])
    S(stD, "el2", [128, E])
    S(stD, "getile", [128, E])
    S(stD, "routw_f", [128, 16 * E])

    dma("qs", sb["normw"][:, :], normw_d.ap(), (), (("normw",),))
    for j in range(4):
        r("pe", lambda e, j=j: e.matmul(ps["b7"][:, :512], sb["ones1f"][:1, :128],
                                        sb["normw"][:1, j * 512:(j + 1) * 512],
                                        start=True, stop=True),
          (("ones1f",), ("normw",)), (("b7",),))
        r("act", lambda e, j=j: e.copy(sb["normw_rep"][:, j * 512:(j + 1) * 512],
                                       ps["b7"][:, :512]),
          (("b7",),), (("normw_rep",),))
    dma("qs", sb["routw_f"][:, :].rearrange("p (c e) -> p c e", e=E),
        routerw_d.ap().rearrange("(c p) e -> p c e", p=128), (), (("routw_f",),))

    for tt in range(NCH):
        orow = sb["o_tm"][:, tt * VD:(tt + 1) * VD]
        ord_ = tuple(("o_tm", tt, h) for h in range(NVH))
        zsrc = z_in if split == "B" else z_sp
        dma("qs", sb["ztile"][:, :], zsrc.ap()[tt * 128:(tt + 1) * 128, :],
            tuple(("zsp", tt, zt) for zt in range(16)) + (("ztile", "w"),),
            (("ztile", "w"),))
        for h in range(NVH):
            r("dve", lambda e, orow=orow, h=h: e.tensor_mul(
                sb["sqh"][:, :], orow[:, h * 128:(h + 1) * 128],
                orow[:, h * 128:(h + 1) * 128]),
              ord_, (("sqh",),))
            r("dve", lambda e, h=h: e.tensor_reduce(
                sb["ssq"][:, h:h + 1], sb["sqh"][:, :], AX.X, AL.add),
              (("sqh",),), (("ssq", h),))
        r("act", lambda e: e.activation(sb["rstd16"][:, :], sb["ssq"][:, :], AF.Ln,
                                        scale=1.0 / DV, bias=1e-6),
          tuple(("ssq", h) for h in range(NVH)) + (("cvals",),), (("rstd16",),))
        r("act", lambda e: e.activation(sb["rstd16"][:, :], sb["rstd16"][:, :],
                                        AF.Exp, scale=-0.5),
          (("rstd16",),), (("rstd16",),))
        for h in range(NVH):
            r("dve", lambda e, orow=orow, h=h: e.tensor_scalar(
                sb["tmp2k"][:, h * 128:(h + 1) * 128], orow[:, h * 128:(h + 1) * 128],
                sb["rstd16"][:, h:h + 1], None, AL.mult),
              ord_ + (("rstd16",),), (("tmp2k", h),))
        tk = tuple(("tmp2k", h) for h in range(NVH))
        r("dve", lambda e: e.tensor_mul(sb["tmp2k"][:, :], sb["tmp2k"][:, :],
                                        sb["normw_rep"][:, :]),
          tk + (("normw_rep",),), tk)
        r("act", lambda e: e.activation(sb["tmp2kb"][:, :], sb["ztile"][:, :], AF.Silu),
          (("ztile", "w"),), (("tmp2kb",),))
        r("dve", lambda e: e.tensor_mul(sb["gat_b"][:, :], sb["tmp2k"][:, :],
                                        sb["tmp2kb"][:, :]),
          tk + (("tmp2kb",),), (("gat_b",),))
        for vt in range(16):
            r("pe", lambda e, vt=vt: e.transpose(
                ps["b4"][:, :128], sb["gat_b"][:, vt * 128:(vt + 1) * 128],
                sb["identb"][:, :]),
              (("gat_b",), ("identb",)), (("b4",),))
            r("act", lambda e, vt=vt, tt=tt: e.copy(
                sb["gatedT"][:, vt * TOK + tt * 128: vt * TOK + (tt + 1) * 128],
                ps["b4"][:, :128]),
              (("b4",),), (("gatedT", vt, tt),))

    for oc in range(4):
        for vt in range(16):
            dma("qp", sb["wopc"][:, vt * 512:(vt + 1) * 512],
                wopT_d.ap()[vt * 128:(vt + 1) * 128, oc * 512:(oc + 1) * 512],
                (("wopc", vt),), (("wopc", vt),))
        for tt in range(NCH):
            sl = tt % 2
            bank = "b0" if tt % 2 == 0 else "b1"
            dma("qs", sb["xps"][:, sl * 512:(sl + 1) * 512],
                xtm.ap()[tt * 128:(tt + 1) * 128, oc * 512:(oc + 1) * 512],
                (("xps", sl),), (("xps", sl, "w"),))
            for vt in range(16):
                r("pe", lambda e, vt=vt, tt=tt, bank=bank: e.matmul(
                    ps[bank][:, :512],
                    sb["gatedT"][:, vt * TOK + tt * 128: vt * TOK + (tt + 1) * 128],
                    sb["wopc"][:, vt * 512:(vt + 1) * 512],
                    start=(vt == 0), stop=(vt == 15)),
                  (("gatedT", vt, tt), ("wopc", vt)), ((bank,),))
            r("dve", lambda e, sl=sl, bank=bank: e.tensor_add(
                sb["ev5"][:, sl * 512:(sl + 1) * 512], ps[bank][:, :512],
                sb["xps"][:, sl * 512:(sl + 1) * 512]),
              ((bank,), ("xps", sl, "w")), (("ev5", sl), ("xps", sl)))
            dma("qs", x1pre.ap()[tt * 128:(tt + 1) * 128, oc * 512:(oc + 1) * 512],
                sb["ev5"][:, sl * 512:(sl + 1) * 512],
                (("ev5", sl),), (("x1pre", tt, oc),))

    for tt in range(NCH):
        dma("qs", sb["tmp2k"][:, :], x1pre.ap()[tt * 128:(tt + 1) * 128, :],
            tuple(("x1pre", tt, oc) for oc in range(4)) +
            tuple(("tmp2k", "x1", oc) for oc in range(4)),
            tuple(("tmp2k", "x1", oc) for oc in range(4)))
        x1k = tuple(("tmp2k", "x1", oc) for oc in range(4))
        r("dve", lambda e: e.tensor_reduce(sb["red1"][:, :], sb["tmp2k"][:, :],
                                           AX.X, AL.add),
          x1k, (("red1",),))
        r("dve", lambda e: e.tensor_scalar_mul(sb["red1"][:, :], sb["red1"][:, :],
                                               1.0 / H),
          (("red1",),), (("red1",),))
        r("dve", lambda e: e.tensor_scalar(sb["tmp2k"][:, :], sb["tmp2k"][:, :],
                                           sb["red1"][:, :], None, AL.subtract),
          x1k + (("red1",),), x1k)
        r("dve", lambda e: e.tensor_mul(sb["tmp2kb"][:, :], sb["tmp2k"][:, :],
                                        sb["tmp2k"][:, :]),
          x1k, (("tmp2kb",),))
        r("dve", lambda e: e.tensor_reduce(sb["red1b"][:, :], sb["tmp2kb"][:, :],
                                           AX.X, AL.add),
          (("tmp2kb",),), (("red1b",),))
        r("act", lambda e: e.activation(sb["red1b"][:, :], sb["red1b"][:, :], AF.Ln,
                                        scale=1.0 / H, bias=1e-5),
          (("red1b",), ("cvals",)), (("red1b",),))
        r("act", lambda e: e.activation(sb["red1b"][:, :], sb["red1b"][:, :],
                                        AF.Exp, scale=-0.5),
          (("red1b",),), (("red1b",),))
        r("dve", lambda e: e.tensor_scalar(sb["tmp2k"][:, :], sb["tmp2k"][:, :],
                                           sb["red1b"][:, :], None, AL.mult),
          x1k + (("red1b",),), x1k)
        sl1 = tt % 2
        r("pool", lambda e, sl1=sl1: e.tensor_copy(
            sb["x1t"][:, sl1 * H:(sl1 + 1) * H], sb["tmp2k"][:, :]),
          x1k, (("x1t", sl1),))
        dma("qp", x1_sp.ap()[tt * 128:(tt + 1) * 128, :],
            sb["x1t"][:, sl1 * H:(sl1 + 1) * H], (("x1t", sl1),), (("x1sp", tt),))
        for hcp in range(16):
            r("pe", lambda e, hcp=hcp: e.transpose(
                ps["b2"][:, :128], sb["tmp2k"][:, hcp * 128:(hcp + 1) * 128],
                sb["identf"][:, :]),
              x1k + (("identf",),), (("b2",),))
            r("act", lambda e, hcp=hcp: e.copy(
                sb["xlnTf"][:, hcp * 128:(hcp + 1) * 128], ps["b2"][:, :128]),
              (("b2",),), (("xlnTf", hcp),))
            r("pool", lambda e, hcp=hcp, tt=tt: e.tensor_copy(
                sb["xlnT"][:, hcp * TOK + tt * 128: hcp * TOK + (tt + 1) * 128],
                sb["xlnTf"][:, hcp * 128:(hcp + 1) * 128]),
              (("xlnTf", hcp),), (("xlnT", hcp, tt),))
        for hcp in range(16):
            r("pe", lambda e, hcp=hcp: e.matmul(
                ps["b5"][:, :E], sb["xlnTf"][:, hcp * 128:(hcp + 1) * 128],
                sb["routw_f"][:, hcp * E:(hcp + 1) * E],
                start=(hcp == 0), stop=(hcp == 15)),
              (("xlnTf", hcp), ("routw_f",)), (("b5",),))
        r("dve", lambda e: e.reduce_max(sb["red1c"][:, :], ps["b5"][:, :E], AX.X),
          (("b5",),), (("red1c",),))
        r("dve", lambda e: e.tensor_scalar_mul(sb["red1c"][:, :], sb["red1c"][:, :],
                                               -1.0),
          (("red1c",),), (("red1c",),))
        r("act", lambda e: e.activation(sb["eltile"][:, :], ps["b5"][:, :E], AF.Exp,
                                        bias=sb["red1c"][:, :]),
          (("b5",), ("red1c",)), (("eltile",),))
        r("dve", lambda e: e.tensor_copy(sb["el2"][:, :], sb["eltile"][:, :]),
          (("eltile",),), (("el2",),))
        for _ in range(3):
            r("dve", lambda e: e.reduce_max(sb["red1c"][:, :], sb["el2"][:, :], AX.X),
              (("el2",),), (("red1c",),))
            r("dve", lambda e: e.tensor_scalar(sb["getile"][:, :], sb["el2"][:, :],
                                               sb["red1c"][:, :], None, AL.is_ge),
              (("el2",), ("red1c",)), (("getile",),))
            r("dve", lambda e: e.scalar_tensor_tensor(
                sb["el2"][:, :], sb["getile"][:, :], -1e30, sb["el2"][:, :],
                AL.mult, AL.add),
              (("getile",), ("el2",)), (("el2",),))
        r("dve", lambda e: e.reduce_max(sb["red1c"][:, :], sb["el2"][:, :], AX.X),
          (("el2",),), (("red1c",),))
        r("dve", lambda e: e.tensor_scalar(sb["getile"][:, :], sb["eltile"][:, :],
                                           sb["red1c"][:, :], None, AL.is_ge),
          (("eltile",), ("red1c",)), (("getile",),))
        r("dve", lambda e: e.tensor_mul(sb["eltile"][:, :], sb["eltile"][:, :],
                                        sb["getile"][:, :]),
          (("eltile",), ("getile",)), (("eltile",),))
        r("dve", lambda e: e.tensor_reduce(sb["red1c"][:, :], sb["eltile"][:, :],
                                           AX.X, AL.add),
          (("eltile",),), (("red1c",),))
        r("dve", lambda e: e.reciprocal(sb["red1c"][:, :], sb["red1c"][:, :]),
          (("red1c",),), (("red1c",),))
        r("dve", lambda e, tt=tt: e.tensor_scalar(
            sb["combine"][:, tt * E:(tt + 1) * E], sb["eltile"][:, :],
            sb["red1c"][:, :], None, AL.mult),
          (("eltile",), ("red1c",)), (("combine", tt),))

    B.barrier()
    stD.close()
    if DBG_STOP_AFTER == "P7":
        _finish(B, nc, sems, sb, dbg_outs, out_d, early=True)
        return nc

    stF = contextlib.ExitStack()
    S(stF, "ffn_acc", [128, NCH * H])
    stE = contextlib.ExitStack()
    S(stE, "wbf_a", [128, 2048], BF16)
    S(stE, "wbf_b", [128, 2048], BF16)
    S(stE, "wdbf", [128, 6 * 512], BF16)
    S(stE, "sgt", [128, 2 * 512])
    S(stE, "hmid", [128, 6 * TOK], BF16)
    S(stE, "crep_e", [128, TOK], BF16)
    S(stE, "crow_e", [1, 2 * 128])

    r("dve", lambda e: e.memset(sb["ffn_acc"][:, :], 0.0), (),
      tuple(("ffn_acc", tt, oc) for tt in range(NCH) for oc in range(4)))

    stSM = contextlib.ExitStack()
    S(stSM, "smid", [128, 32 * 512], BF16)
    for half in range(2):
        th0 = half * 512
        for it in range(SI // 128):
            for wname, wdr, wbuf in (("sg", wsg_d, "wbf_a"),
                                     ("su", wsu_d, "wbf_b")):
                dma("qp", sb[wbuf][:, :].rearrange("p (c i) -> p c i", i=128),
                    wdr.ap()[it], ((wbuf,),), ((wbuf,),))
            bg = "b0" if it % 2 == 0 else "b1"
            bu = "b2" if it % 2 == 0 else "b3"
            xk = tuple(("xlnT", hcp, tt) for hcp in range(16)
                       for tt in range(4 * half, 4 * half + 4))
            for hcp in range(16):
                r("pe", lambda e, hcp=hcp, th0=th0, bg=bg: e.matmul(
                    ps[bg][:, :512], sb["wbf_a"][:, hcp * 128:(hcp + 1) * 128],
                    sb["xlnT"][:, hcp * TOK + th0: hcp * TOK + th0 + 512],
                    start=(hcp == 0), stop=(hcp == 15)),
                  (("wbf_a",),) + xk, ((bg,),))
            for hcp in range(16):
                r("pe", lambda e, hcp=hcp, th0=th0, bu=bu: e.matmul(
                    ps[bu][:, :512], sb["wbf_b"][:, hcp * 128:(hcp + 1) * 128],
                    sb["xlnT"][:, hcp * TOK + th0: hcp * TOK + th0 + 512],
                    start=(hcp == 0), stop=(hcp == 15)),
                  (("wbf_b",),) + xk, ((bu,),))
            sgs = it % 2
            r("act", lambda e, sgs=sgs, bg=bg: e.activation(
                sb["sgt"][:, sgs * 512:(sgs + 1) * 512], ps[bg][:, :512], AF.Silu),
              ((bg,),), (("sgt", sgs),))
            r("dve", lambda e, it=it, sgs=sgs, bu=bu: e.tensor_mul(
                sb["smid"][:, it * 512:(it + 1) * 512],
                sb["sgt"][:, sgs * 512:(sgs + 1) * 512], ps[bu][:, :512]),
              (("sgt", sgs), (bu,)), (("smid", it),))
        for oc in range(4):
            banks = ["b0", "b1", "b5", "b7"]
            for sic in range(SI // 128):
                slw = sic % 2
                dma("qp", sb["wdbf"][:, slw * 512:(slw + 1) * 512],
                    wsd_d.ap()[sic * 128:(sic + 1) * 128, oc * 512:(oc + 1) * 512],
                    (("wdbf", slw),), (("wdbf", slw),))
                for tl in range(4):
                    tt = half * 4 + tl
                    r("pe", lambda e, sic=sic, slw=slw, tl=tl, bk=banks[tl]: e.matmul(
                        ps[bk][:, :512],
                        sb["smid"][:, sic * 512 + tl * 128: sic * 512 + (tl + 1) * 128],
                        sb["wdbf"][:, slw * 512:(slw + 1) * 512],
                        start=(sic == 0), stop=(sic == SI // 128 - 1)),
                      (("smid", sic), ("wdbf", slw)), ((banks[tl],),))
            for tl in range(4):
                tt = half * 4 + tl
                r("dve", lambda e, tt=tt, oc=oc, bk=banks[tl]: e.tensor_add(
                    sb["ffn_acc"][:, tt * H + oc * 512: tt * H + (oc + 1) * 512],
                    sb["ffn_acc"][:, tt * H + oc * 512: tt * H + (oc + 1) * 512],
                    ps[bk][:, :512]),
                  ((banks[tl],),), (("ffn_acc", tt, oc),))
    B.barrier()
    stSM.close()
    if DBG_STOP_AFTER == "P8":
        _finish(B, nc, sems, sb, dbg_outs, out_d, early=True)
        return nc

    for e_i in range(E):
        for tt in range(NCH):
            csl = tt % 2
            r("pe", lambda e, e_i=e_i, tt=tt: e.transpose(
                ps["b2"][:1, :128],
                sb["combine"][:, tt * E + e_i: tt * E + e_i + 1], sb["identf"][:, :]),
              (("combine", tt), ("identf",)), (("b2",),))
            r("dve", lambda e, csl=csl: e.tensor_copy(
                sb["crow_e"][:1, csl * 128:(csl + 1) * 128], ps["b2"][:1, :128]),
              (("b2",),), (("crow_e", csl),))
            r("pe", lambda e, csl=csl: e.matmul(
                ps["b3"][:, :128], sb["ones1f"][:1, :128],
                sb["crow_e"][:1, csl * 128:(csl + 1) * 128], start=True, stop=True),
              (("crow_e", csl), ("ones1f",)), (("b3",),))
            r("act", lambda e, tt=tt: e.copy(
                sb["crep_e"][:, tt * 128:(tt + 1) * 128], ps["b3"][:, :128]),
              (("b3",),), (("crep_e", tt),))
        crk = tuple(("crep_e", tt) for tt in range(NCH))
        for it in range(6):
            for wname, wdr, wbuf in (("eg", wg_d, "wbf_a"),
                                     ("eu", wu_d, "wbf_b")):
                dma("qp", sb[wbuf][:, :].rearrange("p (c i) -> p c i", i=128),
                    wdr.ap()[e_i, it], ((wbuf,),), ((wbuf,),))
            for tch in range(2):
                th0 = tch * 512
                bg = "b0" if tch == 0 else "b1"
                bu = "b5" if tch == 0 else "b7"
                xk = tuple(("xlnT", hcp, tt) for hcp in range(16) for tt in range(8))
                for hcp in range(16):
                    r("pe", lambda e, hcp=hcp, th0=th0, bg=bg: e.matmul(
                        ps[bg][:, :512], sb["wbf_a"][:, hcp * 128:(hcp + 1) * 128],
                        sb["xlnT"][:, hcp * TOK + th0: hcp * TOK + th0 + 512],
                        start=(hcp == 0), stop=(hcp == 15)),
                      (("wbf_a",),) + xk, ((bg,),))
                for hcp in range(16):
                    r("pe", lambda e, hcp=hcp, th0=th0, bu=bu: e.matmul(
                        ps[bu][:, :512], sb["wbf_b"][:, hcp * 128:(hcp + 1) * 128],
                        sb["xlnT"][:, hcp * TOK + th0: hcp * TOK + th0 + 512],
                        start=(hcp == 0), stop=(hcp == 15)),
                      (("wbf_b",),) + xk, ((bu,),))
                sgs = tch
                r("act", lambda e, sgs=sgs, bg=bg: e.activation(
                    sb["sgt"][:, sgs * 512:(sgs + 1) * 512], ps[bg][:, :512], AF.Silu),
                  ((bg,),), (("sgt", sgs),))
                r("dve", lambda e, sgs=sgs, bu=bu: e.tensor_mul(
                    sb["sgt"][:, sgs * 512:(sgs + 1) * 512],
                    sb["sgt"][:, sgs * 512:(sgs + 1) * 512], ps[bu][:, :512]),
                  (("sgt", sgs), (bu,)), (("sgt", sgs),))
                r("dve", lambda e, it=it, th0=th0, sgs=sgs: e.tensor_mul(
                    sb["hmid"][:, it * TOK + th0: it * TOK + th0 + 512],
                    sb["sgt"][:, sgs * 512:(sgs + 1) * 512],
                    sb["crep_e"][:, th0:th0 + 512]),
                  (("sgt", sgs),) + crk, (("hmid", it),))
        for oc in range(4):
            for ic in range(6):
                dma("qp", sb["wdbf"][:, ic * 512:(ic + 1) * 512],
                    wd_d.ap()[e_i, ic * 128:(ic + 1) * 128, oc * 512:(oc + 1) * 512],
                    (("wdbf", ic),), (("wdbf", ic),))
            for tt in range(NCH):
                bk = "b2" if tt % 2 == 0 else "b3"
                for ic in range(6):
                    r("pe", lambda e, ic=ic, tt=tt, bk=bk: e.matmul(
                        ps[bk][:, :512],
                        sb["hmid"][:, ic * TOK + tt * 128: ic * TOK + (tt + 1) * 128],
                        sb["wdbf"][:, ic * 512:(ic + 1) * 512],
                        start=(ic == 0), stop=(ic == 5)),
                      (("hmid", ic), ("wdbf", ic)), ((bk,),))
                r("dve", lambda e, tt=tt, oc=oc, bk=bk: e.tensor_add(
                    sb["ffn_acc"][:, tt * H + oc * 512: tt * H + (oc + 1) * 512],
                    sb["ffn_acc"][:, tt * H + oc * 512: tt * H + (oc + 1) * 512],
                    ps[bk][:, :512]),
                  ((bk,),), (("ffn_acc", tt, oc),))

    B.barrier()
    stE.close()
    stDE.close()
    stG = contextlib.ExitStack()
    S(stG, "x2t", [128, H])
    S(stG, "sq2k", [128, H])
    S(stG, "xrt", [128, 2 * H], BF16)
    for tt in range(NCH):
        sl = tt % 2
        dma("qs", sb["xrt"][:, sl * H:(sl + 1) * H],
            x1_sp.ap()[tt * 128:(tt + 1) * 128, :],
            (("xrt", sl), ("x1sp", tt)), (("xrt", sl),))
        fk = tuple(("ffn_acc", tt, oc) for oc in range(4))
        r("dve", lambda e, tt=tt, sl=sl: e.tensor_add(
            sb["x2t"][:, :], sb["ffn_acc"][:, tt * H:(tt + 1) * H],
            sb["xrt"][:, sl * H:(sl + 1) * H]),
          fk + (("xrt", sl),), (("x2t",),))
        r("dve", lambda e: e.tensor_reduce(sb["red1"][:, :], sb["x2t"][:, :],
                                           AX.X, AL.add),
          (("x2t",),), (("red1",),))
        r("dve", lambda e: e.tensor_scalar_mul(sb["red1"][:, :], sb["red1"][:, :],
                                               1.0 / H),
          (("red1",),), (("red1",),))
        r("dve", lambda e: e.tensor_scalar(sb["x2t"][:, :], sb["x2t"][:, :],
                                           sb["red1"][:, :], None, AL.subtract),
          (("x2t",), ("red1",)), (("x2t",),))
        r("dve", lambda e: e.tensor_mul(sb["sq2k"][:, :], sb["x2t"][:, :],
                                        sb["x2t"][:, :]),
          (("x2t",),), (("sq2k",),))
        r("dve", lambda e: e.tensor_reduce(sb["red1b"][:, :], sb["sq2k"][:, :],
                                           AX.X, AL.add),
          (("sq2k",),), (("red1b",),))
        r("act", lambda e: e.activation(sb["red1b"][:, :], sb["red1b"][:, :], AF.Ln,
                                        scale=1.0 / H, bias=1e-5),
          (("red1b",), ("cvals",)), (("red1b",),))
        r("act", lambda e: e.activation(sb["red1b"][:, :], sb["red1b"][:, :],
                                        AF.Exp, scale=-0.5),
          (("red1b",),), (("red1b",),))
        r("dve", lambda e: e.tensor_scalar(sb["x2t"][:, :], sb["x2t"][:, :],
                                           sb["red1b"][:, :], None, AL.mult),
          (("x2t",), ("red1b",)), (("x2t", "f"),))
        dma("qs", out_d.ap()[tt * 128:(tt + 1) * 128, :], sb["x2t"][:, :],
            (("x2t", "f"),), (("outw", tt),))

    _finish(B, nc, sems, sb, dbg_outs, out_d, early=False)
    return nc



def _finish(B, nc, sems, sb, dbg_outs, out_d, early):
    r = B.rec
    for (oname, sname, sl) in DBG_TAPS:
        src = sb[sname][sl[0]:sl[1], sl[2]:sl[3]]
        dr = dbg_outs[oname]
        keys = tuple(k for k in list(B.writers.keys())
                     if isinstance(k, tuple) and len(k) > 0 and k[0] == sname)
        r("qp", lambda e, dr=dr, src=src: e.dma_start(out=dr.ap(), in_=src),
          keys, ((oname,),))
    if early:
        r("qs", lambda e: e.dma_start(out=out_d.ap()[0:128, 0:128],
                                      in_=sb["identf"][:, :]),
          (("identf",),), (("outw", 0),))
    r("pool", lambda e: e.engine_nop(),
      tuple(("outw", tt) for tt in range(NCH)) +
      tuple((o,) for o in dbg_outs), ())
    B.finalize(sems)


def _stage_inputs(inputs):
    x = np.ascontiguousarray(inputs["L0_attn_x"].reshape(8192, H).astype(np.float32))
    xT_full = np.ascontiguousarray(x.T)
    xT_pad = np.concatenate([np.zeros((H, 3), np.float32), xT_full], axis=1)
    wip = np.asarray(inputs["L0_attn_in_proj_w"], np.float32)
    wipT = np.zeros((H, IPD), np.float32)
    wipT[:, :CD + VD] = wip[:CD + VD].T
    wipT[:, CD + VD:CD + VD + 16] = wip[CD + VD:CD + VD + 16].T
    wipT[:, CD + VD + 128:CD + VD + 144] = wip[CD + VD + 16:CD + VD + 32].T
    convw = np.ascontiguousarray(
        np.asarray(inputs["L0_attn_conv_w"], np.float32)
        .reshape(24, 128, KS).transpose(1, 0, 2).reshape(128, 24 * KS))
    wopT = np.ascontiguousarray(np.asarray(inputs["L0_attn_out_proj_w"], np.float32).T)
    wg = np.ascontiguousarray(np.asarray(inputs["L0_ffn_w_gate"], np.float32)
                              .reshape(E, 16, 128, 6, 128).transpose(0, 3, 2, 1, 4))
    wu = np.ascontiguousarray(np.asarray(inputs["L0_ffn_w_up"], np.float32)
                              .reshape(E, 16, 128, 6, 128).transpose(0, 3, 2, 1, 4))
    wsg = np.ascontiguousarray(np.asarray(inputs["L0_ffn_ws_gate"], np.float32)
                               .reshape(16, 128, 32, 128).transpose(2, 1, 0, 3))
    wsu = np.ascontiguousarray(np.asarray(inputs["L0_ffn_ws_up"], np.float32)
                               .reshape(16, 128, 32, 128).transpose(2, 1, 0, 3))
    identf = np.eye(128, dtype=np.float32)
    identb = np.eye(128).astype(BF)
    ones1f = np.ones((1, 128), np.float32)
    ones2b = np.zeros((128, 2), np.float32)
    ones2b[:64, 0] = 1.0
    ones2b[64:, 1] = 1.0
    blk2f = np.zeros((2, 128), np.float32)
    blk2f[0, :64] = 1.0
    blk2f[1, 64:] = 1.0
    tt_, ss_ = np.meshgrid(np.arange(128), np.arange(128), indexing="ij")
    maskS = np.where(ss_ >= tt_, 1e30, 0.0).astype(np.float32)
    maskI = np.where(ss_ > tt_, 1e30, 0.0).astype(np.float32)
    normw = np.ascontiguousarray(
        np.tile(np.asarray(inputs["L0_attn_norm_weight"], np.float32), 16)[None, :])
    shared = {
        "wipT": wipT, "convw": convw,
        "dtb": np.asarray(inputs["L0_attn_dt_bias"], np.float32).reshape(16, 1),
        "alog": np.asarray(inputs["L0_attn_A_log"], np.float32).reshape(16, 1),
        "normw": normw, "wopT": wopT,
        "routerw": np.ascontiguousarray(np.asarray(inputs["L0_ffn_router_w"], np.float32)),
        "wg": wg, "wu": wu,
        "wd": np.ascontiguousarray(np.asarray(inputs["L0_ffn_w_down"], np.float32)),
        "wsg": wsg, "wsu": wsu,
        "wsd": np.ascontiguousarray(np.asarray(inputs["L0_ffn_ws_down"], np.float32)),
        "cvals": np.tile(np.array([[1.0, 1e-6, 1e-5]], np.float32), (128, 1)),
        "identf": identf, "identb": identb, "ones1f": ones1f,
        "ones2b": ones2b.astype(BF), "blk2f": blk2f, "maskS": maskS, "maskI": maskI,
    }
    in_maps = []
    for c in range(NCORE):
        t0 = c * TOK
        m = dict(shared)
        m["xT"] = np.ascontiguousarray(xT_pad[:, t0:t0 + TOKE])
        m["xtm"] = np.ascontiguousarray(x[t0:t0 + TOK])
        segmask = np.zeros((64, NCORE), np.float32)
        segmask[:, :c] = 1.0
        m["segmask"] = segmask
        in_maps.append(m)
    return in_maps


def kernel(**inputs):
    in_maps = _stage_inputs(inputs)
    cores = list(range(NCORE))
    ncA = build_program("A")
    namesA = set(ncA._din)
    mapsA = [{k: v for k, v in m.items() if k in namesA} for m in in_maps]
    resA = run_bass_kernel_spmd(ncA, mapsA, core_ids=cores)
    segall = np.ascontiguousarray(np.concatenate(
        [np.asarray(resA.results[c]["segout"]) for c in range(NCORE)], axis=1))
    ncB = build_program("B")
    namesB = set(ncB._din)
    mapsB = []
    for c in range(NCORE):
        m = {k: v for k, v in in_maps[c].items() if k in namesB}
        ra = resA.results[c]
        m["segall_in"] = segall
        m["pt_in"] = np.asarray(ra["pt_o"])
        m["g_in"] = np.asarray(ra["g_o"])
        m["lo_in"] = np.asarray(ra["lo_o"])
        m["otm_in"] = np.asarray(ra["otm_o"])
        m["tc_in"] = np.asarray(ra["tc_o"])
        m["z_in"] = np.asarray(ra["zspill"])
        mapsB.append(m)
    resB = run_bass_kernel_spmd(ncB, mapsB, core_ids=cores)
    out = np.concatenate([np.asarray(r["out"]) for r in resB.results], axis=0)
    return out.reshape(1, 8192, H).astype(np.float32)


# version 08 (final)
# speedup vs baseline: 1.2121x; 1.2121x over previous
import contextlib
import numpy as np
import ml_dtypes
import concourse.bass as bass
import concourse.mybir as mybir
from concourse.bass_utils import run_bass_kernel_spmd

F32 = mybir.dt.float32
BF16 = mybir.dt.bfloat16
AL = mybir.AluOpType
AF = mybir.ActivationFunctionType
AX = mybir.AxisListType

NCORE = 8
TOK = 1024
TOKE = TOK + 3
H = 2048
NKH, NVH, DK, DV, KS = 8, 16, 64, 128, 4
KD, VD = NKH * DK, NVH * DV
CD = 2 * KD + VD
IPD = 5376
C = 128
NCH = TOK // C
E, I_MOE, SI = 32, 768, 4096
SCALE = DK ** -0.5
BF = ml_dtypes.bfloat16

DBG_TAPS = []
DBG_STOP_AFTER = None
DBG_P1_OTILES = None

SEQ_OF = {"pe": "tensor", "act": "scalar", "dve": "vector", "pool": "gpsimd",
          "qs": "sync", "qa": "scalar", "qp": "gpsimd", "cc": "gpsimd"}
DMA_V = ("qs", "qa", "qp", "cc")


class Builder:
    def __init__(self, nc):
        self.nc = nc
        self.ops = []
        self.writers = {}
        self.readers = {}
        self.deps = []
        self.marked = set()
        self.bar = []

    def rec(self, virt, fn, reads=(), writes=()):
        i = len(self.ops)
        dep = set(self.bar)
        for k in reads:
            w = self.writers.get(k)
            if w is not None:
                dep.add(w)
        for k in writes:
            w = self.writers.get(k)
            if w is not None:
                dep.add(w)
            for rd in self.readers.get(k, ()):
                dep.add(rd)
        self.ops.append((SEQ_OF[virt], virt, fn))
        self.deps.append(dep)
        for k in reads:
            self.readers.setdefault(k, []).append(i)
        for k in writes:
            self.writers[k] = i
            self.readers[k] = []
        return i

    def barrier(self):
        last = {}
        for i in range(len(self.ops) - 1, -1, -1):
            v = self.ops[i][1]
            if v not in last:
                last[v] = i
            if len(last) == 8:
                break
        self.bar = list(last.values())

    def finalize(self, sems):
        R = 4
        n = len(self.ops)
        virt_of = [o[1] for o in self.ops]
        real_deps = []
        for i in range(n):
            rd = []
            for d in self.deps[i]:
                if virt_of[d] == virt_of[i] == "pe":
                    continue
                if d == i:
                    continue
                rd.append(d)
                self.marked.add(d)
            real_deps.append(rd)
        slot = {}
        seq_in_slot = {}
        counts = {}
        inc_after = [0] * n
        for i in range(n):
            v = virt_of[i]
            if v in DMA_V:
                k = counts.get(v, 0)
                counts[v] = k + 1
                slot[i] = k % R
                seq_in_slot[i] = k // R
            elif i in self.marked:
                counts[v] = counts.get(v, 0) + 1
            inc_after[i] = counts.get(v, 0)
        per_seq = {}
        for i in range(n):
            per_seq.setdefault(self.ops[i][0], []).append(i)
        nc = self.nc

        def dep_wait(d):
            v = virt_of[d]
            if v in DMA_V:
                return (sems[v][slot[d]], 16 * (seq_in_slot[d] + 1))
            return (sems[v], inc_after[d])

        def make_body(seq):
            idxs = per_seq.get(seq, [])

            def body(eng):
                waited = {}
                for i in idxs:
                    wl = [dep_wait(d) for d in real_deps[i]]
                    v = virt_of[i]
                    if v in DMA_V and seq_in_slot[i] > 0:
                        wl.append((sems[v][slot[i]], 16 * seq_in_slot[i]))
                    for s, val in wl:
                        key = id(s)
                        if val > waited.get(key, -1):
                            eng.wait_ge(s, val)
                            waited[key] = val
                    inst = self.ops[i][2](eng)
                    if v in DMA_V:
                        inst.then_inc(sems[v][slot[i]], 16)
                    elif i in self.marked:
                        inst.then_inc(sems[v], 1)
            return body

        with nc.Block() as block:
            block.sync(make_body("sync"))
            block.tensor(make_body("tensor"))
            block.scalar(make_body("scalar"))
            block.vector(make_body("vector"))
            block.gpsimd(make_body("gpsimd"))


def build_program(split=None):
    nc = bass.Bass("TRN2", target_bir_lowering=False)
    B = Builder(nc)
    gst = contextlib.ExitStack()
    nc._gst = gst
    sb = {}

    def S(stk, name, shape, dtype=F32, side=None):
        kw = {"side": side} if side else {}
        sb[name] = stk.enter_context(
            nc.sbuf_tensor('s_' + name, list(shape), dtype, **kw))
        return sb[name]

    din = {}

    def DI(name, shape, dtype=F32):
        din[name] = nc.dram_tensor(name, list(shape), dtype, kind="ExternalInput")
        return din[name]
    nc._din = din

    if split != "B":
        xT = DI("xT", [H, TOKE])
        wipT = DI("wipT", [H, IPD])
        convw_d = DI("convw", [128, 24 * KS])
        dtb_d = DI("dtb", [16, 1])
        alog_d = DI("alog", [16, 1])
    if split != "A":
        xtm = DI("xtm", [TOK, H])
        normw_d = DI("normw", [1, H])
        wopT_d = DI("wopT", [VD, H])
        routerw_d = DI("routerw", [H, E])
        wg_d = DI("wg", [E, 6, 128, 16, 128])
        wu_d = DI("wu", [E, 6, 128, 16, 128])
        wd_d = DI("wd", [E, I_MOE, H])
        wsg_d = DI("wsg", [SI // 128, 128, 16, 128])
        wsu_d = DI("wsu", [SI // 128, 128, 16, 128])
        wsd_d = DI("wsd", [SI, H])
        seg_in_d = DI("segall_in", [64, NCORE * NVH * 192])
        pt_in = DI("pt_in", [64, NVH * NCH * 64], BF16)
        g_in = DI("g_in", [64, NVH * NCH * 128], BF16)
        lo_in = DI("lo_in", [64, NVH * NCH * 128], BF16)
        otm_in = DI("otm_in", [128, NCH * VD], BF16)
        tc_in = DI("tc_in", [128, NCH * 64])
        z_in = DI("z_in", [TOK, VD], BF16)
    identf_d = DI("identf", [128, 128])
    identb_d = DI("identb", [128, 128], BF16)
    ones1f_d = DI("ones1f", [1, 128])
    ones2b_d = DI("ones2b", [128, 2], BF16)
    blk2f_d = DI("blk2f", [2, 128])
    maskS_d = DI("maskS", [128, 128])
    maskI_d = DI("maskI", [128, 128])
    segmask_d = DI("segmask", [64, NCORE])
    cvals_d = DI("cvals", [128, 3])

    out_d = nc.dram_tensor("out", [TOK, H], F32, kind="ExternalOutput")
    if split == "A":
        seg_o = nc.dram_tensor("segout", [64, NVH * 192], F32, kind="ExternalOutput")
        pt_o = nc.dram_tensor("pt_o", [64, NVH * NCH * 64], BF16, kind="ExternalOutput")
        g_o = nc.dram_tensor("g_o", [64, NVH * NCH * 128], BF16, kind="ExternalOutput")
        lo_o = nc.dram_tensor("lo_o", [64, NVH * NCH * 128], BF16, kind="ExternalOutput")
        otm_o = nc.dram_tensor("otm_o", [128, NCH * VD], BF16, kind="ExternalOutput")
        tc_o = nc.dram_tensor("tc_o", [128, NCH * 64], F32, kind="ExternalOutput")
    cc_in = nc.dram_tensor("cc_in", [NVH * 64, 192], F32, kind="Internal")
    cc_out = nc.dram_tensor("cc_out", [NCORE * NVH * 64, 192], F32, kind="Internal", addr_space="Shared")
    x1_sp = nc.dram_tensor("x1spill", [TOK, H], BF16, kind="Internal")
    x1pre = nc.dram_tensor("x1pre", [TOK, H], F32, kind="Internal")
    z_sp = nc.dram_tensor("zspill", [TOK, VD], BF16,
                      kind="ExternalOutput" if split == "A" else "Internal")
    dbg_outs = {}
    for (oname, sname, sl) in DBG_TAPS:
        dbg_outs[oname] = nc.dram_tensor(
            oname, [sl[1] - sl[0], sl[3] - sl[2]], F32, kind="ExternalOutput")

    ps = {}
    for nm in ("b0", "b1", "b2", "b3", "b5", "b7"):
        ps[nm] = gst.enter_context(nc.psum_tensor(nm, [128, 512], F32))
    ps["b4"] = gst.enter_context(nc.psum_tensor("b4", [128, 1024], BF16))
    ps["b6"] = gst.enter_context(nc.psum_tensor("b6", [128, 1024], BF16))

    sems = {}
    for v in ("pe", "act", "dve", "pool"):
        sems[v] = gst.enter_context(nc.semaphore(f"s_{v}"))
    for v in ("qs", "qa", "qp", "cc"):
        sems[v] = [gst.enter_context(nc.semaphore(f"s_{v}{j}")) for j in range(4)]

    r = B.rec

    def dma(virt, dst_ap, src_ap, reads, writes):
        r(virt, lambda e, d=dst_ap, s=src_ap: e.dma_start(out=d, in_=s),
          reads, writes)

    for nm, shp, dt in [("identf", [128, 128], F32), ("identb", [128, 128], BF16),
                        ("ones1f", [1, 128], F32), ("ones2b", [128, 2], BF16),
                        ("blk2f", [2, 128], F32), ("maskS", [128, 128], F32),
                        ("maskI", [128, 128], F32), ("segmask", [64, NCORE], F32),
                        ("convw", [128, 24 * KS], F32), ("dtb", [16, 1], F32),
                        ("alog", [16, 1], F32), ("nexpA", [16, 1], F32),
                        ("tcols", [128, NCH * 64], F32),
                        ("o_tm", [128, NCH * VD], BF16),
                        ("combine", [128, NCH * E], F32),
                        ("red1", [128, 1], F32), ("red1b", [128, 1], F32),
                        ("red1c", [128, 1], F32), ("cvals", [128, 3], F32)]:
        S(gst, nm, shp, dt)

    p0 = [("identf", identf_d), ("identb", identb_d), ("ones1f", ones1f_d),
          ("ones2b", ones2b_d), ("blk2f", blk2f_d), ("maskS", maskS_d),
          ("maskI", maskI_d), ("segmask", segmask_d), ("cvals", cvals_d)]
    if split != "B":
        p0 += [("convw", convw_d), ("dtb", dtb_d), ("alog", alog_d)]
    for nm, dr in p0:
        dma("qs", sb[nm][:, :], dr.ap(), (), ((nm,),))
    nc.const_aps.register(F32, 1.0, sb["cvals"][:, 0:1])
    nc.const_aps.register(F32, 1e-6, sb["cvals"][:, 1:2])
    nc.const_aps.register(F32, 1e-5, sb["cvals"][:, 2:3])
    if split != "B":
        r("act", lambda e: e.activation(sb["nexpA"][:16, :], sb["alog"][:16, :],
                                        AF.Exp),
          (("alog",),), (("nexpA",),))
        r("dve", lambda e: e.tensor_scalar_mul(sb["nexpA"][:16, :],
                                               sb["nexpA"][:16, :], -1.0),
          (("nexpA",),), (("nexpA",),))

    if DBG_STOP_AFTER == "P0":
        _finish(B, nc, sems, sb, dbg_outs, out_d, early=True)
        return nc

    if split != "B":
        stAB = contextlib.ExitStack()
        S(stAB, "q_bf", [128, 4 * TOK], BF16, side="right")
        S(stAB, "k_bf", [128, 4 * TOK], BF16, side="right")
        S(stAB, "v_bf", [128, 16 * TOK], BF16, side="right")

        stA = contextlib.ExitStack()
        S(stA, "xfm", [128, 16 * 1028], BF16)
        S(stA, "wipb", [128, 2 * 16 * 512], BF16)
        S(stA, "stage", [128, 2 * TOKE])
        S(stA, "qkvt", [128, TOK])
        S(stA, "sqb", [128, TOK], BF16)
        S(stA, "rr", [2, TOK])
        S(stA, "gt", [16, TOK])
        S(stA, "ct16", [16, TOK])
        S(stA, "bt16", [16, TOK])
        S(stA, "blt16", [16, TOK])
        S(stA, "nbt16", [16, TOK])
        S(stA, "lamt", [16, TOK])
        S(stA, "stk64", [64, TOK], side="right")
        S(stA, "spt", [16, 2 * TOK])
        S(stA, "ztr", [128, 128], BF16)

        for hc in range(16):
            dma("qp", sb["xfm"][:, hc * 1028: hc * 1028 + TOKE],
                xT.ap()[hc * 128:(hc + 1) * 128, :], (), (("xfm", hc),))

        ogs = [(i * 4, 4) for i in range(10)] + [(40, 2)]
        if DBG_P1_OTILES is not None:
            no = DBG_P1_OTILES
            ogs = [(o0, min(olen, no - o0)) for (o0, olen) in ogs if o0 < no]
        for og, (o0, olen) in enumerate(ogs):
            sl = og % 2
            ow = olen * 128
            for hc in range(16):
                dma("qp", sb["wipb"][:, (sl * 16 + hc) * 512:(sl * 16 + hc) * 512 + ow],
                    wipT.ap()[hc * 128:(hc + 1) * 128, o0 * 128: o0 * 128 + ow],
                    (("wipb", sl, hc),), (("wipb", sl, hc),))
            for otl in range(olen):
                ot = o0 + otl
                ssl = ot % 2
                for tch, (t0, tl) in enumerate([(0, 512), (512, 512), (1024, 3)]):
                    bank = "b0" if (ot * 3 + tch) % 2 == 0 else "b1"
                    for hc in range(16):
                        r("pe", lambda e, hc=hc, sl=sl, otl=otl, t0=t0, tl=tl, bank=bank:
                          e.matmul(ps[bank][:, :tl],
                                   sb["wipb"][:, (sl * 16 + hc) * 512 + otl * 128:
                                              (sl * 16 + hc) * 512 + (otl + 1) * 128],
                                   sb["xfm"][:, hc * 1028 + t0: hc * 1028 + t0 + tl],
                                   start=(hc == 0), stop=(hc == 15)),
                          (("wipb", sl, hc), ("xfm", hc)), ((bank,),))
                    r("act", lambda e, ssl=ssl, t0=t0, tl=tl, bank=bank: e.copy(
                        sb["stage"][:, ssl * TOKE + t0: ssl * TOKE + t0 + tl],
                        ps[bank][:, :tl]),
                      ((bank,),), (("stage", ssl, tch),))
                stg = sb["stage"][:, ssl * TOKE:(ssl + 1) * TOKE]
                srd = (("stage", ssl, 0), ("stage", ssl, 1), ("stage", ssl, 2))
                if ot < 24:
                    qsl = 0
                    qv = sb["qkvt"][:, 0:TOK]
                    r("dve", lambda e, stg=stg, qv=qv, ot=ot: e.tensor_scalar(
                        qv, stg[:, 0:TOK], sb["convw"][:, ot * 4:ot * 4 + 1], None, AL.mult),
                      srd + (("convw",),), (("qkvt", qsl),))
                    for k in (1, 2, 3):
                        r("dve", lambda e, stg=stg, qv=qv, ot=ot, k=k:
                          e.scalar_tensor_tensor(qv, stg[:, k:k + TOK],
                                                 sb["convw"][:, ot * 4 + k:ot * 4 + k + 1],
                                                 qv, AL.mult, AL.add),
                          srd + (("qkvt", qsl), ("convw",)), (("qkvt", qsl),))
                    r("act", lambda e, qv=qv: e.activation(qv, qv, AF.Silu),
                      (("qkvt", qsl),), (("qkvt", qsl),))
                    if ot < 8:
                        isq = ot < 4
                        dst = sb["q_bf"] if isq else sb["k_bf"]
                        dn = "q_bf" if isq else "k_bf"
                        ti = ot if isq else ot - 4
                        r("dve", lambda e, qv=qv: e.tensor_mul(sb["sqb"][:, :], qv, qv),
                          (("qkvt", qsl),), (("sqb",),))
                        for hf, bank in ((0, "b2"), (1, "b3")):
                            r("pe", lambda e, hf=hf, bank=bank: e.matmul(
                                ps[bank][:2, :512], sb["ones2b"][:, :2],
                                sb["sqb"][:, hf * 512:(hf + 1) * 512],
                                start=True, stop=True),
                              (("sqb",), ("ones2b",)), ((bank,),))
                            r("act", lambda e, hf=hf, bank=bank: e.copy(
                                sb["rr"][:2, hf * 512:(hf + 1) * 512], ps[bank][:2, :512]),
                              ((bank,),), (("rr", hf),))
                        rrv = sb["rr"][:2, 0:TOK]
                        r("dve", lambda e, rrv=rrv: e.tensor_scalar_max(rrv, rrv, 1e-24),
                          (("rr", 0), ("rr", 1)), (("rr", 0), ("rr", 1)))
                        r("act", lambda e, rrv=rrv: e.activation(rrv, rrv, AF.Ln),
                          (("rr", 0), ("rr", 1)), (("rr", 0), ("rr", 1)))
                        r("act", lambda e, rrv=rrv: e.activation(rrv, rrv, AF.Exp, scale=-0.5),
                          (("rr", 0), ("rr", 1)), (("rr", 0), ("rr", 1)))
                        if isq:
                            r("dve", lambda e, rrv=rrv: e.tensor_scalar_mul(rrv, rrv, SCALE),
                              (("rr", 0), ("rr", 1)), (("rr", 0), ("rr", 1)))
                        for hf, bank in ((0, "b2"), (1, "b3")):
                            r("pe", lambda e, hf=hf, bank=bank: e.matmul(
                                ps[bank][:, :512], sb["blk2f"][:2, :],
                                sb["rr"][:2, hf * 512:(hf + 1) * 512],
                                start=True, stop=True),
                              (("rr", 0), ("rr", 1), ("blk2f",)), ((bank,),))
                            r("dve", lambda e, qv=qv, dst=dst, ti=ti, hf=hf, bank=bank:
                              e.tensor_mul(
                                  dst[:, ti * TOK + hf * 512: ti * TOK + (hf + 1) * 512],
                                  qv[:, hf * 512:(hf + 1) * 512], ps[bank][:, :512]),
                              ((bank,), ("qkvt", qsl)), ((dn, ti),))
                    else:
                        vt = ot - 8
                        r("pool", lambda e, qv=qv, vt=vt: e.tensor_copy(
                            sb["v_bf"][:, vt * TOK:(vt + 1) * TOK], qv),
                          (("qkvt", qsl),), (("v_bf", vt),))
                elif ot < 40:
                    zt = ot - 24
                    for ch in range(NCH):
                        r("pe", lambda e, stg=stg, ch=ch: e.transpose(
                            ps["b2"][:, :128], stg[:, 3 + ch * 128: 3 + (ch + 1) * 128],
                            sb["identf"][:, :]),
                          srd + (("identf",),), (("b2",),))
                        r("act", lambda e: e.copy(sb["ztr"][:, :], ps["b2"][:, :128]),
                          (("b2",),), (("ztr",),))
                        dma("qs", z_sp.ap()[ch * 128:(ch + 1) * 128,
                                            zt * 128:(zt + 1) * 128],
                            sb["ztr"][:, :], (("ztr",),), (("zsp", ch, zt),))
                elif ot == 40:
                    r("act", lambda e, stg=stg: e.activation(
                        sb["bt16"][:16, :], stg[:16, 3:TOKE], AF.Sigmoid),
                      srd, (("stk", "beta"),))
                else:
                    sp = sb["spt"]
                    r("dve", lambda e, stg=stg, sp=sp: e.tensor_scalar(
                        sp[:16, 0:TOK], stg[:16, 3:TOKE], sb["dtb"][:16, :], None, AL.add),
                      srd + (("dtb",),), (("spt", 0),))
                    r("dve", lambda e, sp=sp: e.tensor_scalar_mul(
                        sp[:16, TOK:2 * TOK], sp[:16, 0:TOK], -1.0),
                      (("spt", 0),), (("spt", 1),))
                    r("dve", lambda e, sp=sp: e.tensor_max(
                        sp[:16, TOK:2 * TOK], sp[:16, 0:TOK], sp[:16, TOK:2 * TOK]),
                      (("spt", 0), ("spt", 1)), (("spt", 1),))
                    r("act", lambda e, sp=sp: e.activation(
                        sp[:16, TOK:2 * TOK], sp[:16, TOK:2 * TOK], AF.Exp, scale=-1.0),
                      (("spt", 1),), (("spt", 1),))
                    r("act", lambda e, sp=sp: e.activation(
                        sp[:16, TOK:2 * TOK], sp[:16, TOK:2 * TOK], AF.Ln, bias=1.0),
                      (("spt", 1), ("cvals",)), (("spt", 1),))
                    r("dve", lambda e, sp=sp: e.tensor_scalar_max(
                        sb["gt"][:16, :], sp[:16, 0:TOK], 0.0),
                      (("spt", 0),), (("gt",),))
                    r("dve", lambda e, sp=sp: e.tensor_add(
                        sb["gt"][:16, :], sb["gt"][:16, :], sp[:16, TOK:2 * TOK]),
                      (("spt", 1), ("gt",)), (("gt",),))
                    r("dve", lambda e: e.tensor_scalar(
                        sb["gt"][:16, :], sb["gt"][:16, :], sb["nexpA"][:16, :], None, AL.mult),
                      (("gt",), ("nexpA",)), (("gt",),))

        if DBG_P1_OTILES is not None:
            _finish(B, nc, sems, sb, dbg_outs, out_d, early=True)
            return nc
        r("dve", lambda e: e.tensor_copy(sb["spt"][:16, 0:TOK], sb["gt"][:16, :]),
          (("gt",), ("spt", 0)), (("spt", 0),))
        cur, oth = 0, 1
        for s in (1, 2, 4, 8, 16, 32, 64):
            for ch in range(NCH):
                r("dve", lambda e, ch=ch, s=s, cur=cur, oth=oth: e.tensor_add(
                    sb["spt"][:16, oth * TOK + ch * C + s: oth * TOK + (ch + 1) * C],
                    sb["spt"][:16, cur * TOK + ch * C + s: cur * TOK + (ch + 1) * C],
                    sb["spt"][:16, cur * TOK + ch * C: cur * TOK + (ch + 1) * C - s]),
                  (("spt", cur),), (("spt", oth),))
                r("dve", lambda e, ch=ch, s=s, cur=cur, oth=oth: e.tensor_copy(
                    sb["spt"][:16, oth * TOK + ch * C: oth * TOK + ch * C + s],
                    sb["spt"][:16, cur * TOK + ch * C: cur * TOK + ch * C + s]),
                  (("spt", cur),), (("spt", oth),))
            cur, oth = oth, cur
        r("dve", lambda e, cur=cur: e.tensor_copy(sb["ct16"][:16, :],
                                                  sb["spt"][:16, cur * TOK:(cur + 1) * TOK]),
          (("spt", cur),), (("stk", "c"),))
        r("act", lambda e: e.activation(sb["lamt"][:16, :], sb["ct16"][:16, :], AF.Exp),
          (("stk", "c"),), (("lamt",),))
        r("dve", lambda e: e.tensor_mul(sb["blt16"][:16, :], sb["bt16"][:16, :],
                                        sb["lamt"][:16, :]),
          (("stk", "beta"), ("lamt",)), (("stk", "bl"),))
        r("dve", lambda e: e.tensor_scalar_mul(sb["nbt16"][:16, :], sb["bt16"][:16, :], -1.0),
          (("stk", "beta"),), (("stk", "nb"),))
        for gi, (nm, key) in enumerate([("ct16", ("stk", "c")),
                                        ("bt16", ("stk", "beta")),
                                        ("blt16", ("stk", "bl")),
                                        ("nbt16", ("stk", "nb"))]):
            dma("qs", sb["stk64"][gi * 16:(gi + 1) * 16, :], sb[nm][:16, :],
                (key,), (("stk64",),))
        for ch in range(NCH):
            r("pe", lambda e, ch=ch: e.transpose(
                ps["b2"][:, :64], sb["stk64"][:64, ch * C:(ch + 1) * C],
                sb["identf"][:64, :64]),
              (("stk64",), ("identf",)), (("b2",),))
            r("act", lambda e, ch=ch: e.copy(
                sb["tcols"][:, ch * 64:(ch + 1) * 64], ps["b2"][:, :64]),
              (("b2",),), (("tcols", ch),))

        B.barrier()
        stA.close()
        if DBG_STOP_AFTER == "P1":
            _finish(B, nc, sems, sb, dbg_outs, out_d, early=True)
            return nc

    stBC = contextlib.ExitStack()
    S(stBC, "PT", [64, NVH * NCH * 64], BF16)
    S(stBC, "G", [64, NVH * NCH * 128], BF16)
    S(stBC, "LO", [64, NVH * NCH * 128], BF16)

    if split == "B":
        dma("qs", sb["PT"][:, :], pt_in.ap(),
            (), tuple(("PT", h, ch) for h in range(NVH) for ch in range(NCH)))
        dma("qs", sb["G"][:, :], g_in.ap(),
            (), tuple(("G", h, ch) for h in range(NVH) for ch in range(NCH)))
        dma("qs", sb["LO"][:, :], lo_in.ap(),
            (), tuple(("LO", h, ch) for h in range(NVH) for ch in range(NCH)))
        dma("qs", sb["o_tm"][:, :], otm_in.ap(),
            (), tuple(("o_tm", ch, h) for ch in range(NCH) for h in range(NVH)))
        dma("qs", sb["tcols"][:, :], tc_in.ap(),
            (), tuple(("tcols", ch) for ch in range(NCH)))

    if split != "B":
        stB = contextlib.ExitStack()
        for nm, shp, dt in [("crow", [1, 128], F32), ("cCcol", [128, 1], F32),
                            ("cdp", [128, 128], F32), ("cdm", [128, 128], F32),
                            ("Dst", [128, 128], F32), ("Din", [128, 128], F32),
                            ("Bb", [128, 128], BF16), ("Mb", [128, 128], BF16),
                            ("Btb", [128, 128], BF16), ("Mtb", [128, 128], BF16),
                            ("Ktm", [128, 64], BF16), ("Vtm", [128, 128], BF16),
                            ("Xf", [128, 192], F32), ("Xb", [128, 192], BF16),
                            ("Pa", [128, 2 * 128], BF16), ("Pta", [128, 2 * 128], BF16),
                            ("kplog", [128, 1], F32), ("kpcol", [128, 1], F32),
                            ("lamc", [128, 1], F32), ("Kp", [128, 64], BF16),
                            ("lamrep", [64, 128], F32), ("t1", [64, 128], F32),
                            ("qb0", [64, 128], BF16)]:
            S(stB, nm, shp, dt)

        def qslice(h):
            kh = h // 2
            return kh // 2, 64 * (kh % 2)

        for ch in range(NCH):
            for h in range(NVH):
                ccol = sb["tcols"][:, ch * 64 + h: ch * 64 + h + 1]
                bcol = sb["tcols"][:, ch * 64 + 16 + h: ch * 64 + 16 + h + 1]
                blcol = sb["tcols"][:, ch * 64 + 32 + h: ch * 64 + 32 + h + 1]
                nbcol = sb["tcols"][:, ch * 64 + 48 + h: ch * 64 + 48 + h + 1]
                ti, pb = qslice(h)
                qv = sb["q_bf"][pb:pb + 64, ti * TOK + ch * C: ti * TOK + (ch + 1) * C]
                kv = sb["k_bf"][pb:pb + 64, ti * TOK + ch * C: ti * TOK + (ch + 1) * C]
                tc = ("tcols", ch)
                r("pe", lambda e, ccol=ccol: e.transpose(ps["b2"][:1, :128], ccol,
                                                         sb["identf"][:, :]),
                  (tc, ("identf",)), (("b2",),))
                r("dve", lambda e: e.tensor_copy(sb["crow"][:1, :], ps["b2"][:1, :128]),
                  (("b2",),), (("crow",),))
                r("pe", lambda e: e.matmul(ps["b2"][:, :1], sb["ones1f"][:1, :128],
                                           sb["crow"][:1, 127:128], start=True, stop=True),
                  (("crow",), ("ones1f",)), (("b2",),))
                r("dve", lambda e: e.tensor_copy(sb["cCcol"][:, :], ps["b2"][:, :1]),
                  (("b2",),), (("cCcol",),))
                r("pe", lambda e: e.matmul(ps["b2"][:, :128], sb["ones1f"][:1, :128],
                                           sb["crow"][:1, :128], start=True, stop=True),
                  (("crow",), ("ones1f",)), (("b2",),))
                r("dve", lambda e, ccol=ccol: e.tensor_scalar(
                    sb["cdp"][:, :], ps["b2"][:, :128], ccol, None, AL.subtract),
                  (("b2",), tc), (("cdp",),))
                r("pe", lambda e, kv=kv: e.matmul(ps["b3"][:, :128], kv, kv,
                                                  start=True, stop=True),
                  (("k_bf", ti),), (("b3",),))
                r("pe", lambda e, qv=qv, kv=kv: e.matmul(ps["b5"][:, :128], qv, kv,
                                                         start=True, stop=True),
                  (("q_bf", ti), ("k_bf", ti)), (("b5",),))
                r("dve", lambda e: e.tensor_add(sb["cdm"][:, :], sb["cdp"][:, :],
                                                sb["maskS"][:, :]),
                  (("cdp",), ("maskS",)), (("cdm",),))
                r("act", lambda e: e.activation(sb["Dst"][:, :], sb["cdm"][:, :],
                                                AF.Exp, scale=-1.0),
                  (("cdm",),), (("Dst",),))
                r("dve", lambda e, nbcol=nbcol: e.scalar_tensor_tensor(
                    sb["Bb"][:, :], sb["Dst"][:, :], nbcol, ps["b3"][:, :128],
                    AL.mult, AL.mult),
                  (("Dst",), tc, ("b3",)), (("Bb",),))
                r("dve", lambda e: e.tensor_add(sb["cdm"][:, :], sb["cdp"][:, :],
                                                sb["maskI"][:, :]),
                  (("cdp",), ("maskI",)), (("cdm",),))
                r("act", lambda e: e.activation(sb["Din"][:, :], sb["cdm"][:, :],
                                                AF.Exp, scale=-1.0),
                  (("cdm",),), (("Din",),))
                r("dve", lambda e: e.tensor_mul(sb["Mb"][:, :], sb["Din"][:, :],
                                                ps["b5"][:, :128]),
                  (("Din",), ("b5",)), (("Mb",),))
                r("pe", lambda e: e.transpose(ps["b4"][:, :128], sb["Bb"][:, :],
                                              sb["identb"][:, :]),
                  (("Bb",), ("identb",)), (("b4",),))
                r("act", lambda e: e.copy(sb["Btb"][:, :], ps["b4"][:, :128]),
                  (("b4",),), (("Btb",),))
                r("pe", lambda e: e.transpose(ps["b6"][:, :128], sb["Mb"][:, :],
                                              sb["identb"][:, :]),
                  (("Mb",), ("identb",)), (("b6",),))
                r("act", lambda e: e.copy(sb["Mtb"][:, :], ps["b6"][:, :128]),
                  (("b6",),), (("Mtb",),))
                r("pe", lambda e, kv=kv, pb=pb: e.transpose(
                    ps["b4"][:, :64], kv, sb["identb"][pb:pb + 64, pb:pb + 64]),
                  (("k_bf", ti), ("identb",)), (("b4",),))
                r("act", lambda e: e.copy(sb["Ktm"][:, :], ps["b4"][:, :64]),
                  (("b4",),), (("Ktm",),))
                r("pe", lambda e, h=h, ch=ch: e.transpose(
                    ps["b6"][:, :128],
                    sb["v_bf"][:, h * TOK + ch * C: h * TOK + (ch + 1) * C],
                    sb["identb"][:, :]),
                  (("v_bf", h), ("identb",)), (("b6",),))
                r("act", lambda e: e.copy(sb["Vtm"][:, :], ps["b6"][:, :128]),
                  (("b6",),), (("Vtm",),))
                r("dve", lambda e, blcol=blcol: e.tensor_scalar(
                    sb["Xf"][:, 0:64], sb["Ktm"][:, :], blcol, None, AL.mult),
                  (("Ktm",), tc), (("Xf",),))
                r("dve", lambda e, bcol=bcol: e.tensor_scalar(
                    sb["Xf"][:, 64:192], sb["Vtm"][:, :], bcol, None, AL.mult),
                  (("Vtm",), tc), (("Xf",),))
                r("pool", lambda e: e.tensor_copy(sb["Xb"][:, :], sb["Xf"][:, :]),
                  (("Xf",),), (("Xb",),))
                pk, ptk = ("Bb",), ("Btb",)
                pv, ptv = sb["Bb"][:, :], sb["Btb"][:, :]
                for it in range(7):
                    bx = "b0" if it % 2 == 0 else "b1"
                    r("pe", lambda e, ptv=ptv, bx=bx: e.matmul(
                        ps[bx][:, :192], ptv, sb["Xb"][:, :192], start=True, stop=True),
                      (ptk, ("Xb",)), ((bx,),))
                    r("dve", lambda e, bx=bx: e.tensor_add(sb["Xf"][:, :], sb["Xf"][:, :],
                                                           ps[bx][:, :192]),
                      ((bx,), ("Xf",)), (("Xf",),))
                    r("pool", lambda e: e.tensor_copy(sb["Xb"][:, :], sb["Xf"][:, :]),
                      (("Xf",),), (("Xb",),))
                    if it < 6:
                        sl2 = it % 2
                        r("pe", lambda e, ptv=ptv, pv=pv: e.matmul(
                            ps["b7"][:, :128], ptv, pv, start=True, stop=True),
                          (ptk, pk), (("b7",),))
                        r("act", lambda e, sl2=sl2: e.copy(
                            sb["Pa"][:, sl2 * 128:(sl2 + 1) * 128], ps["b7"][:, :128]),
                          (("b7",),), (("Pa", sl2),))
                        r("pe", lambda e, sl2=sl2: e.transpose(
                            ps["b4"][:, :128], sb["Pa"][:, sl2 * 128:(sl2 + 1) * 128],
                            sb["identb"][:, :]),
                          (("Pa", sl2), ("identb",)), (("b4",),))
                        r("act", lambda e, sl2=sl2: e.copy(
                            sb["Pta"][:, sl2 * 128:(sl2 + 1) * 128], ps["b4"][:, :128]),
                          (("b4",),), (("Pta", sl2),))
                        pk, ptk = ("Pa", sl2), ("Pta", sl2)
                        pv = sb["Pa"][:, sl2 * 128:(sl2 + 1) * 128]
                        ptv = sb["Pta"][:, sl2 * 128:(sl2 + 1) * 128]
                r("dve", lambda e, ccol=ccol: e.tensor_sub(sb["kplog"][:, :],
                                                           sb["cCcol"][:, :], ccol),
                  (("cCcol",), tc), (("kplog",),))
                r("act", lambda e: e.activation(sb["kpcol"][:, :], sb["kplog"][:, :], AF.Exp),
                  (("kplog",),), (("kpcol",),))
                r("act", lambda e: e.activation(sb["lamc"][:, :], sb["cCcol"][:, :], AF.Exp),
                  (("cCcol",),), (("lamc",),))
                r("dve", lambda e: e.tensor_scalar(sb["Kp"][:, :], sb["Ktm"][:, :],
                                                   sb["kpcol"][:, :], None, AL.mult),
                  (("Ktm",), ("kpcol",)), (("Kp",),))
                r("pe", lambda e: e.matmul(ps["b3"][:64, :64], sb["Xb"][:, 0:64],
                                           sb["Kp"][:, :], start=True, stop=True),
                  (("Xb",), ("Kp",)), (("b3",),))
                PTd = sb["PT"][:, (h * NCH + ch) * 64:(h * NCH + ch + 1) * 64]
                r("dve", lambda e, PTd=PTd: e.scalar_tensor_tensor(
                    PTd, sb["identf"][:64, :64], sb["lamc"][:64, :], ps["b3"][:64, :64],
                    AL.mult, AL.subtract),
                  (("identf",), ("lamc",), ("b3",)), (("PT", h, ch),))
                r("pe", lambda e: e.matmul(ps["b5"][:64, :128], sb["Kp"][:, :],
                                           sb["Xb"][:, 64:192], start=True, stop=True),
                  (("Kp",), ("Xb",)), (("b5",),))
                Gd = sb["G"][:, (h * NCH + ch) * 128:(h * NCH + ch + 1) * 128]
                r("act", lambda e, Gd=Gd: e.copy(Gd, ps["b5"][:64, :128]),
                  (("b5",),), (("G", h, ch),))
                r("pe", lambda e: e.matmul(ps["b7"][:64, :128], sb["Xb"][:, 0:64],
                                           sb["Mtb"][:, :], start=True, stop=True),
                  (("Xb",), ("Mtb",)), (("b7",),))
                r("pe", lambda e: e.matmul(ps["b2"][:64, :128], sb["ones1f"][:1, :64],
                                           sb["crow"][:1, :128], start=True, stop=True),
                  (("crow",), ("ones1f",)), (("b2",),))
                r("act", lambda e: e.activation(sb["lamrep"][:, :], ps["b2"][:64, :128],
                                                AF.Exp),
                  (("b2",),), (("lamrep",),))
                r("act", lambda e, qv=qv: e.copy(sb["qb0"][:, :], qv),
                  (("q_bf", ti),), (("qb0",),))
                r("dve", lambda e: e.tensor_mul(sb["t1"][:, :], sb["qb0"][:, :],
                                                sb["lamrep"][:, :]),
                  (("qb0",), ("lamrep",)), (("t1",),))
                LOd = sb["LO"][:, (h * NCH + ch) * 128:(h * NCH + ch + 1) * 128]
                r("dve", lambda e, LOd=LOd: e.tensor_sub(LOd, sb["t1"][:, :],
                                                         ps["b7"][:64, :128]),
                  (("t1",), ("b7",)), (("LO", h, ch),))
                bo = "b0" if h % 2 == 0 else "b1"
                r("pe", lambda e, bo=bo: e.matmul(ps[bo][:, :128], sb["Mtb"][:, :],
                                                  sb["Xb"][:, 64:192], start=True, stop=True),
                  (("Mtb",), ("Xb",)), ((bo,),))
                r("dve", lambda e, bo=bo, ch=ch, h=h: e.tensor_copy(
                    sb["o_tm"][:, ch * VD + h * 128: ch * VD + (h + 1) * 128],
                    ps[bo][:, :128]),
                  ((bo,),), (("o_tm", ch, h),))

        B.barrier()
        stB.close()
        stAB.close()
        if DBG_STOP_AFTER == "P3":
            _finish(B, nc, sems, sb, dbg_outs, out_d, early=True)
            return nc

    stC = contextlib.ExitStack()
    S(stC, "segsb", [64, NVH * 192])
    S(stC, "segall", [64, NVH * 192])
    S(stC, "Sg", [64, 128])
    S(stC, "Sgb", [64, 128], BF16)
    S(stC, "Mc", [64, 2 * 64], BF16)
    S(stC, "Sin", [64, NVH * 128])
    S(stC, "Sbf", [64, NVH * 128], BF16)
    S(stC, "ptp", [64, 64], BF16)
    S(stC, "gp", [64, 128])
    S(stC, "tt1", [64, 64])

    if split != "B":
      for h in range(NVH):
          for i in range(NCH):
              PTs = sb["PT"][:, (h * NCH + i) * 64:(h * NCH + i + 1) * 64]
              Gs = sb["G"][:, (h * NCH + i) * 128:(h * NCH + i + 1) * 128]
              if i == 0:
                  r("dve", lambda e, Gs=Gs, h=h: e.tensor_copy(sb["Sg"][:, :], Gs),
                    (("G", h, 0),), (("Sg",),))
              else:
                  r("pe", lambda e, PTs=PTs: e.matmul(
                      ps["b2"][:64, :128], PTs, sb["Sgb"][:, :], start=True, stop=True),
                    (("PT", h, i), ("Sgb",)), (("b2",),))
                  r("dve", lambda e, Gs=Gs, h=h, i=i: e.tensor_add(
                      sb["Sg"][:, :], ps["b2"][:64, :128], Gs),
                    (("b2",), ("G", h, i)), (("Sg",),))
              r("pool", lambda e: e.tensor_copy(sb["Sgb"][:, :], sb["Sg"][:, :]),
                (("Sg",),), (("Sgb",),))
              if i == 0:
                  r("pe", lambda e, PTs=PTs: e.transpose(
                      ps["b4"][:64, :64], PTs, sb["identb"][:64, :64]),
                    (("PT", h, 0), ("identb",)), (("b4",),))
                  r("act", lambda e: e.copy(sb["Mc"][:, 0:64], ps["b4"][:64, :64]),
                    (("b4",),), (("Mc", 0),))
              else:
                  src, dst = (i - 1) % 2, i % 2
                  r("pe", lambda e, PTs=PTs, src=src: e.matmul(
                      ps["b3"][:64, :64], PTs, sb["Mc"][:, src * 64:(src + 1) * 64],
                      start=True, stop=True),
                    (("PT", h, i), ("Mc", src)), (("b3",),))
                  r("act", lambda e, dst=dst: e.copy(
                      sb["Mc"][:, dst * 64:(dst + 1) * 64], ps["b3"][:64, :64]),
                    (("b3",),), (("Mc", dst),))
          r("dve", lambda e, h=h: e.tensor_copy(
              sb["segsb"][:, h * 192:h * 192 + 64], sb["Mc"][:, 64:128]),
            (("Mc", 1),), (("segsb", h),))
          r("dve", lambda e, h=h: e.tensor_copy(
              sb["segsb"][:, h * 192 + 64:h * 192 + 192], sb["Sg"][:, :]),
            (("Sg",),), (("segsb", h),))
    if split == "A":
        dma("qs", seg_o.ap(), sb["segsb"][:, :],
            tuple(("segsb", h) for h in range(NVH)), (("outw", 0),))
        dma("qs", pt_o.ap(), sb["PT"][:, :],
            tuple(("PT", h, ch) for h in range(NVH) for ch in range(NCH)),
            (("outw", 1),))
        dma("qs", g_o.ap(), sb["G"][:, :],
            tuple(("G", h, ch) for h in range(NVH) for ch in range(NCH)),
            (("outw", 2),))
        dma("qs", lo_o.ap(), sb["LO"][:, :],
            tuple(("LO", h, ch) for h in range(NVH) for ch in range(NCH)),
            (("outw", 3),))
        dma("qs", otm_o.ap(), sb["o_tm"][:, :],
            tuple(("o_tm", ch, h) for ch in range(NCH) for h in range(NVH)),
            (("outw", 4),))
        dma("qs", tc_o.ap(), sb["tcols"][:, :],
            tuple(("tcols", ch) for ch in range(NCH)), (("outw", 5),))
        r("qs", lambda e: e.dma_start(out=out_d.ap()[0:128, 0:128],
                                      in_=sb["identf"][:, :]),
          (("identf",),), (("outw", 6),))
        r("pool", lambda e: e.engine_nop(),
          tuple(("outw", i) for i in range(7)) +
          tuple(("zsp", ch, zt) for ch in range(NCH) for zt in range(16)), ())
        B.finalize(sems)
        return nc
    if split is None:
        dma("qp", cc_in.ap().rearrange("(h p) f -> p h f", p=64),
            sb["segsb"][:, :].rearrange("p (h f) -> p h f", f=192),
            tuple(("segsb", h) for h in range(NVH)), (("cc_in",),))
        r("cc", lambda e: e.collective_compute(
            "AllGather", AL.bypass, replica_groups=[list(range(NCORE))],
            ins=[cc_in.ap()], outs=[cc_out.ap()]),
          (("cc_in",),), (("cc_out",),))
    r("dve", lambda e: e.memset(sb["Sin"][:, :], 0.0), (),
      tuple(("Sin", h) for h in range(NVH)))
    r("pool", lambda e: e.memset(sb["Sbf"][:, :], 0.0), (),
      tuple(("Sbf", h) for h in range(NVH)))
    for j in range(NCORE - 1):
        if split == "B":
            dma("qs", sb["segall"][:, :],
                seg_in_d.ap()[:, j * NVH * 192:(j + 1) * NVH * 192],
                (("segall",),), (("segall",),))
        else:
            dma("qp", sb["segall"][:, :].rearrange("p (h f) -> p h f", f=192),
                cc_out.ap()[j * 1024:(j + 1) * 1024, :].rearrange(
                    "(h p) f -> p h f", p=64),
                (("cc_out",), ("segall",)), (("segall",),))
        for h in range(NVH):
            base = h * 192
            mcol = sb["segmask"][:, j:j + 1]
            r("pe", lambda e, base=base: e.transpose(
                ps["b2"][:64, :64], sb["segall"][:, base:base + 64],
                sb["identf"][:64, :64]),
              (("segall",), ("identf",)), (("b2",),))
            r("dve", lambda e: e.tensor_sub(sb["tt1"][:, :], ps["b2"][:64, :64],
                                            sb["identf"][:64, :64]),
              (("b2",), ("identf",)), (("tt1",),))
            r("dve", lambda e, mcol=mcol: e.tensor_scalar(
                sb["tt1"][:, :], sb["tt1"][:, :], mcol, None, AL.mult),
              (("tt1",), ("segmask",)), (("tt1",),))
            r("dve", lambda e: e.tensor_add(sb["ptp"][:, :], sb["tt1"][:, :],
                                            sb["identf"][:64, :64]),
              (("tt1",), ("identf",)), (("ptp",),))
            r("dve", lambda e, base=base, mcol=mcol: e.tensor_scalar(
                sb["gp"][:, :], sb["segall"][:, base + 64:base + 192],
                mcol, None, AL.mult),
              (("segall",), ("segmask",)), (("gp",),))
            r("pe", lambda e, h=h: e.matmul(
                ps["b3"][:64, :128], sb["ptp"][:, :],
                sb["Sbf"][:, h * 128:(h + 1) * 128], start=True, stop=True),
              (("ptp",), ("Sbf", h)), (("b3",),))
            r("dve", lambda e, h=h: e.tensor_add(
                sb["Sin"][:, h * 128:(h + 1) * 128], ps["b3"][:64, :128],
                sb["gp"][:, :]),
              (("b3",), ("gp",)), (("Sin", h),))
            r("pool", lambda e, h=h: e.tensor_copy(
                sb["Sbf"][:, h * 128:(h + 1) * 128],
                sb["Sin"][:, h * 128:(h + 1) * 128]),
              (("Sin", h),), (("Sbf", h),))

    for ch in range(NCH):
        for grp in range(4):
            bo = "b0" if grp % 2 == 0 else "b1"
            bs = "b2" if grp % 2 == 0 else "b3"
            hs = list(range(grp * 4, grp * 4 + 4))
            for h in hs:
                LOs = sb["LO"][:, (h * NCH + ch) * 128:(h * NCH + ch + 1) * 128]
                r("pe", lambda e, LOs=LOs, h=h, bo=bo: e.matmul(
                    ps[bo][:, (h % 4) * 128:(h % 4 + 1) * 128], LOs,
                    sb["Sbf"][:, h * 128:(h + 1) * 128], start=True, stop=True),
                  (("LO", h, ch), ("Sbf", h)), ((bo,),))
            for h in hs:
                PTs = sb["PT"][:, (h * NCH + ch) * 64:(h * NCH + ch + 1) * 64]
                r("pe", lambda e, PTs=PTs, h=h, bs=bs: e.matmul(
                    ps[bs][:64, (h % 4) * 128:(h % 4 + 1) * 128], PTs,
                    sb["Sbf"][:, h * 128:(h + 1) * 128], start=True, stop=True),
                  (("PT", h, ch), ("Sbf", h)), ((bs,),))
            for h in hs:
                r("dve", lambda e, h=h, ch=ch, bo=bo: e.tensor_add(
                    sb["o_tm"][:, ch * VD + h * 128: ch * VD + (h + 1) * 128],
                    sb["o_tm"][:, ch * VD + h * 128: ch * VD + (h + 1) * 128],
                    ps[bo][:, (h % 4) * 128:(h % 4 + 1) * 128]),
                  ((bo,), ("o_tm", ch, h)), (("o_tm", ch, h),))
            for h in hs:
                Gs = sb["G"][:, (h * NCH + ch) * 128:(h * NCH + ch + 1) * 128]
                r("dve", lambda e, h=h, Gs=Gs, bs=bs, ch=ch: e.tensor_add(
                    sb["Sin"][:, h * 128:(h + 1) * 128],
                    ps[bs][:64, (h % 4) * 128:(h % 4 + 1) * 128], Gs),
                  ((bs,), ("G", h, ch)), (("Sin", h),))
            for h in hs:
                r("pool", lambda e, h=h: e.tensor_copy(
                    sb["Sbf"][:, h * 128:(h + 1) * 128],
                    sb["Sin"][:, h * 128:(h + 1) * 128]),
                  (("Sin", h),), (("Sbf", h),))

    B.barrier()
    stC.close()
    stBC.close()
    if DBG_STOP_AFTER == "P5":
        _finish(B, nc, sems, sb, dbg_outs, out_d, early=True)
        return nc

    stDE = contextlib.ExitStack()
    S(stDE, "xlnT", [128, 16 * TOK], BF16, side="right")
    stD = contextlib.ExitStack()
    S(stD, "gatedT", [128, 16 * TOK], BF16)
    S(stD, "wopc", [128, 16 * 512], BF16)
    S(stD, "ztile", [128, VD], BF16)
    S(stD, "ssq", [128, 16])
    S(stD, "rstd16", [128, 16])
    S(stD, "sqh", [128, 128])
    S(stD, "tmp2k", [128, H])
    S(stD, "tmp2kb", [128, H])
    S(stD, "gat_b", [128, H], BF16)
    S(stD, "x1t", [128, 2 * H], BF16)
    S(stD, "xps", [128, 2 * 512])
    S(stD, "ev5", [128, 2 * 512])
    S(stD, "normw", [1, H])
    S(stD, "normw_rep", [128, H])
    S(stD, "xlnTf", [128, 16 * 128])
    S(stD, "eltile", [128, E])
    S(stD, "el2", [128, E])
    S(stD, "getile", [128, E])
    S(stD, "routw_f", [128, 16 * E])

    dma("qs", sb["normw"][:, :], normw_d.ap(), (), (("normw",),))
    for j in range(4):
        r("pe", lambda e, j=j: e.matmul(ps["b7"][:, :512], sb["ones1f"][:1, :128],
                                        sb["normw"][:1, j * 512:(j + 1) * 512],
                                        start=True, stop=True),
          (("ones1f",), ("normw",)), (("b7",),))
        r("act", lambda e, j=j: e.copy(sb["normw_rep"][:, j * 512:(j + 1) * 512],
                                       ps["b7"][:, :512]),
          (("b7",),), (("normw_rep",),))
    dma("qs", sb["routw_f"][:, :].rearrange("p (c e) -> p c e", e=E),
        routerw_d.ap().rearrange("(c p) e -> p c e", p=128), (), (("routw_f",),))

    for tt in range(NCH):
        orow = sb["o_tm"][:, tt * VD:(tt + 1) * VD]
        ord_ = tuple(("o_tm", tt, h) for h in range(NVH))
        zsrc = z_in if split == "B" else z_sp
        dma("qs", sb["ztile"][:, :], zsrc.ap()[tt * 128:(tt + 1) * 128, :],
            tuple(("zsp", tt, zt) for zt in range(16)) + (("ztile", "w"),),
            (("ztile", "w"),))
        for h in range(NVH):
            r("dve", lambda e, orow=orow, h=h: e.tensor_mul(
                sb["sqh"][:, :], orow[:, h * 128:(h + 1) * 128],
                orow[:, h * 128:(h + 1) * 128]),
              ord_, (("sqh",),))
            r("dve", lambda e, h=h: e.tensor_reduce(
                sb["ssq"][:, h:h + 1], sb["sqh"][:, :], AX.X, AL.add),
              (("sqh",),), (("ssq", h),))
        r("act", lambda e: e.activation(sb["rstd16"][:, :], sb["ssq"][:, :], AF.Ln,
                                        scale=1.0 / DV, bias=1e-6),
          tuple(("ssq", h) for h in range(NVH)) + (("cvals",),), (("rstd16",),))
        r("act", lambda e: e.activation(sb["rstd16"][:, :], sb["rstd16"][:, :],
                                        AF.Exp, scale=-0.5),
          (("rstd16",),), (("rstd16",),))
        for h in range(NVH):
            r("dve", lambda e, orow=orow, h=h: e.tensor_scalar(
                sb["tmp2k"][:, h * 128:(h + 1) * 128], orow[:, h * 128:(h + 1) * 128],
                sb["rstd16"][:, h:h + 1], None, AL.mult),
              ord_ + (("rstd16",),), (("tmp2k", h),))
        tk = tuple(("tmp2k", h) for h in range(NVH))
        r("dve", lambda e: e.tensor_mul(sb["tmp2k"][:, :], sb["tmp2k"][:, :],
                                        sb["normw_rep"][:, :]),
          tk + (("normw_rep",),), tk)
        r("act", lambda e: e.activation(sb["tmp2kb"][:, :], sb["ztile"][:, :], AF.Silu),
          (("ztile", "w"),), (("tmp2kb",),))
        r("dve", lambda e: e.tensor_mul(sb["gat_b"][:, :], sb["tmp2k"][:, :],
                                        sb["tmp2kb"][:, :]),
          tk + (("tmp2kb",),), (("gat_b",),))
        for vt in range(16):
            r("pe", lambda e, vt=vt: e.transpose(
                ps["b4"][:, :128], sb["gat_b"][:, vt * 128:(vt + 1) * 128],
                sb["identb"][:, :]),
              (("gat_b",), ("identb",)), (("b4",),))
            r("act", lambda e, vt=vt, tt=tt: e.copy(
                sb["gatedT"][:, vt * TOK + tt * 128: vt * TOK + (tt + 1) * 128],
                ps["b4"][:, :128]),
              (("b4",),), (("gatedT", vt, tt),))

    for oc in range(4):
        for vt in range(16):
            dma("qp", sb["wopc"][:, vt * 512:(vt + 1) * 512],
                wopT_d.ap()[vt * 128:(vt + 1) * 128, oc * 512:(oc + 1) * 512],
                (("wopc", vt),), (("wopc", vt),))
        for tt in range(NCH):
            sl = tt % 2
            bank = "b0" if tt % 2 == 0 else "b1"
            dma("qs", sb["xps"][:, sl * 512:(sl + 1) * 512],
                xtm.ap()[tt * 128:(tt + 1) * 128, oc * 512:(oc + 1) * 512],
                (("xps", sl),), (("xps", sl, "w"),))
            for vt in range(16):
                r("pe", lambda e, vt=vt, tt=tt, bank=bank: e.matmul(
                    ps[bank][:, :512],
                    sb["gatedT"][:, vt * TOK + tt * 128: vt * TOK + (tt + 1) * 128],
                    sb["wopc"][:, vt * 512:(vt + 1) * 512],
                    start=(vt == 0), stop=(vt == 15)),
                  (("gatedT", vt, tt), ("wopc", vt)), ((bank,),))
            r("dve", lambda e, sl=sl, bank=bank: e.tensor_add(
                sb["ev5"][:, sl * 512:(sl + 1) * 512], ps[bank][:, :512],
                sb["xps"][:, sl * 512:(sl + 1) * 512]),
              ((bank,), ("xps", sl, "w")), (("ev5", sl), ("xps", sl)))
            dma("qs", x1pre.ap()[tt * 128:(tt + 1) * 128, oc * 512:(oc + 1) * 512],
                sb["ev5"][:, sl * 512:(sl + 1) * 512],
                (("ev5", sl),), (("x1pre", tt, oc),))

    for tt in range(NCH):
        dma("qs", sb["tmp2k"][:, :], x1pre.ap()[tt * 128:(tt + 1) * 128, :],
            tuple(("x1pre", tt, oc) for oc in range(4)) +
            tuple(("tmp2k", "x1", oc) for oc in range(4)),
            tuple(("tmp2k", "x1", oc) for oc in range(4)))
        x1k = tuple(("tmp2k", "x1", oc) for oc in range(4))
        r("dve", lambda e: e.tensor_reduce(sb["red1"][:, :], sb["tmp2k"][:, :],
                                           AX.X, AL.add),
          x1k, (("red1",),))
        r("dve", lambda e: e.tensor_scalar_mul(sb["red1"][:, :], sb["red1"][:, :],
                                               1.0 / H),
          (("red1",),), (("red1",),))
        r("dve", lambda e: e.tensor_scalar(sb["tmp2k"][:, :], sb["tmp2k"][:, :],
                                           sb["red1"][:, :], None, AL.subtract),
          x1k + (("red1",),), x1k)
        r("dve", lambda e: e.tensor_mul(sb["tmp2kb"][:, :], sb["tmp2k"][:, :],
                                        sb["tmp2k"][:, :]),
          x1k, (("tmp2kb",),))
        r("dve", lambda e: e.tensor_reduce(sb["red1b"][:, :], sb["tmp2kb"][:, :],
                                           AX.X, AL.add),
          (("tmp2kb",),), (("red1b",),))
        r("act", lambda e: e.activation(sb["red1b"][:, :], sb["red1b"][:, :], AF.Ln,
                                        scale=1.0 / H, bias=1e-5),
          (("red1b",), ("cvals",)), (("red1b",),))
        r("act", lambda e: e.activation(sb["red1b"][:, :], sb["red1b"][:, :],
                                        AF.Exp, scale=-0.5),
          (("red1b",),), (("red1b",),))
        r("dve", lambda e: e.tensor_scalar(sb["tmp2k"][:, :], sb["tmp2k"][:, :],
                                           sb["red1b"][:, :], None, AL.mult),
          x1k + (("red1b",),), x1k)
        sl1 = tt % 2
        r("pool", lambda e, sl1=sl1: e.tensor_copy(
            sb["x1t"][:, sl1 * H:(sl1 + 1) * H], sb["tmp2k"][:, :]),
          x1k, (("x1t", sl1),))
        dma("qp", x1_sp.ap()[tt * 128:(tt + 1) * 128, :],
            sb["x1t"][:, sl1 * H:(sl1 + 1) * H], (("x1t", sl1),), (("x1sp", tt),))
        for hcp in range(16):
            r("pe", lambda e, hcp=hcp: e.transpose(
                ps["b2"][:, :128], sb["tmp2k"][:, hcp * 128:(hcp + 1) * 128],
                sb["identf"][:, :]),
              x1k + (("identf",),), (("b2",),))
            r("act", lambda e, hcp=hcp: e.copy(
                sb["xlnTf"][:, hcp * 128:(hcp + 1) * 128], ps["b2"][:, :128]),
              (("b2",),), (("xlnTf", hcp),))
            r("pool", lambda e, hcp=hcp, tt=tt: e.tensor_copy(
                sb["xlnT"][:, hcp * TOK + tt * 128: hcp * TOK + (tt + 1) * 128],
                sb["xlnTf"][:, hcp * 128:(hcp + 1) * 128]),
              (("xlnTf", hcp),), (("xlnT", hcp, tt),))
        for hcp in range(16):
            r("pe", lambda e, hcp=hcp: e.matmul(
                ps["b5"][:, :E], sb["xlnTf"][:, hcp * 128:(hcp + 1) * 128],
                sb["routw_f"][:, hcp * E:(hcp + 1) * E],
                start=(hcp == 0), stop=(hcp == 15)),
              (("xlnTf", hcp), ("routw_f",)), (("b5",),))
        r("dve", lambda e: e.reduce_max(sb["red1c"][:, :], ps["b5"][:, :E], AX.X),
          (("b5",),), (("red1c",),))
        r("dve", lambda e: e.tensor_scalar_mul(sb["red1c"][:, :], sb["red1c"][:, :],
                                               -1.0),
          (("red1c",),), (("red1c",),))
        r("act", lambda e: e.activation(sb["eltile"][:, :], ps["b5"][:, :E], AF.Exp,
                                        bias=sb["red1c"][:, :]),
          (("b5",), ("red1c",)), (("eltile",),))
        r("dve", lambda e: e.tensor_copy(sb["el2"][:, :], sb["eltile"][:, :]),
          (("eltile",),), (("el2",),))
        for _ in range(3):
            r("dve", lambda e: e.reduce_max(sb["red1c"][:, :], sb["el2"][:, :], AX.X),
              (("el2",),), (("red1c",),))
            r("dve", lambda e: e.tensor_scalar(sb["getile"][:, :], sb["el2"][:, :],
                                               sb["red1c"][:, :], None, AL.is_ge),
              (("el2",), ("red1c",)), (("getile",),))
            r("dve", lambda e: e.scalar_tensor_tensor(
                sb["el2"][:, :], sb["getile"][:, :], -1e30, sb["el2"][:, :],
                AL.mult, AL.add),
              (("getile",), ("el2",)), (("el2",),))
        r("dve", lambda e: e.reduce_max(sb["red1c"][:, :], sb["el2"][:, :], AX.X),
          (("el2",),), (("red1c",),))
        r("dve", lambda e: e.tensor_scalar(sb["getile"][:, :], sb["eltile"][:, :],
                                           sb["red1c"][:, :], None, AL.is_ge),
          (("eltile",), ("red1c",)), (("getile",),))
        r("dve", lambda e: e.tensor_mul(sb["eltile"][:, :], sb["eltile"][:, :],
                                        sb["getile"][:, :]),
          (("eltile",), ("getile",)), (("eltile",),))
        r("dve", lambda e: e.tensor_reduce(sb["red1c"][:, :], sb["eltile"][:, :],
                                           AX.X, AL.add),
          (("eltile",),), (("red1c",),))
        r("dve", lambda e: e.reciprocal(sb["red1c"][:, :], sb["red1c"][:, :]),
          (("red1c",),), (("red1c",),))
        r("dve", lambda e, tt=tt: e.tensor_scalar(
            sb["combine"][:, tt * E:(tt + 1) * E], sb["eltile"][:, :],
            sb["red1c"][:, :], None, AL.mult),
          (("eltile",), ("red1c",)), (("combine", tt),))

    B.barrier()
    stD.close()
    if DBG_STOP_AFTER == "P7":
        _finish(B, nc, sems, sb, dbg_outs, out_d, early=True)
        return nc

    stF = contextlib.ExitStack()
    S(stF, "ffn_acc", [128, NCH * H])
    stE = contextlib.ExitStack()
    S(stE, "wbf_a", [128, 2 * 2048], BF16)
    S(stE, "wbf_b", [128, 2 * 2048], BF16)
    S(stE, "wdbf", [128, 6 * 512], BF16)
    S(stE, "sgt", [128, 2 * 512])
    S(stE, "hmid", [128, 6 * TOK], BF16)
    S(stE, "crep_e", [128, TOK], BF16)
    S(stE, "crow_e", [1, 2 * 128])

    r("dve", lambda e: e.memset(sb["ffn_acc"][:, :], 0.0), (),
      tuple(("ffn_acc", tt, oc) for tt in range(NCH) for oc in range(4)))

    stSM = contextlib.ExitStack()
    S(stSM, "smid", [128, 32 * 512], BF16)
    for half in range(2):
        th0 = half * 512
        for it in range(SI // 128):
            ws = it % 2
            for wname, wdr, wbuf in (("sg", wsg_d, "wbf_a"),
                                     ("su", wsu_d, "wbf_b")):
                dma("qp", sb[wbuf][:, ws * 2048:(ws + 1) * 2048].rearrange(
                        "p (c i) -> p c i", i=128),
                    wdr.ap()[it], ((wbuf, ws),), ((wbuf, ws),))
            bg = "b0" if it % 2 == 0 else "b1"
            bu = "b2" if it % 2 == 0 else "b3"
            xk = tuple(("xlnT", hcp, tt) for hcp in range(16)
                       for tt in range(4 * half, 4 * half + 4))
            for hcp in range(16):
                r("pe", lambda e, hcp=hcp, th0=th0, bg=bg, ws=ws: e.matmul(
                    ps[bg][:, :512],
                    sb["wbf_a"][:, ws * 2048 + hcp * 128: ws * 2048 + (hcp + 1) * 128],
                    sb["xlnT"][:, hcp * TOK + th0: hcp * TOK + th0 + 512],
                    start=(hcp == 0), stop=(hcp == 15)),
                  (("wbf_a", ws),) + xk, ((bg,),))
            for hcp in range(16):
                r("pe", lambda e, hcp=hcp, th0=th0, bu=bu, ws=ws: e.matmul(
                    ps[bu][:, :512],
                    sb["wbf_b"][:, ws * 2048 + hcp * 128: ws * 2048 + (hcp + 1) * 128],
                    sb["xlnT"][:, hcp * TOK + th0: hcp * TOK + th0 + 512],
                    start=(hcp == 0), stop=(hcp == 15)),
                  (("wbf_b", ws),) + xk, ((bu,),))
            sgs = it % 2
            r("act", lambda e, sgs=sgs, bg=bg: e.activation(
                sb["sgt"][:, sgs * 512:(sgs + 1) * 512], ps[bg][:, :512], AF.Silu),
              ((bg,),), (("sgt", sgs),))
            r("dve", lambda e, it=it, sgs=sgs, bu=bu: e.tensor_mul(
                sb["smid"][:, it * 512:(it + 1) * 512],
                sb["sgt"][:, sgs * 512:(sgs + 1) * 512], ps[bu][:, :512]),
              (("sgt", sgs), (bu,)), (("smid", it),))
        for oc in range(4):
            banks = ["b0", "b1", "b5", "b7"]
            for sic in range(SI // 128):
                slw = sic % 2
                dma("qp", sb["wdbf"][:, slw * 512:(slw + 1) * 512],
                    wsd_d.ap()[sic * 128:(sic + 1) * 128, oc * 512:(oc + 1) * 512],
                    (("wdbf", slw),), (("wdbf", slw),))
                for tl in range(4):
                    tt = half * 4 + tl
                    r("pe", lambda e, sic=sic, slw=slw, tl=tl, bk=banks[tl]: e.matmul(
                        ps[bk][:, :512],
                        sb["smid"][:, sic * 512 + tl * 128: sic * 512 + (tl + 1) * 128],
                        sb["wdbf"][:, slw * 512:(slw + 1) * 512],
                        start=(sic == 0), stop=(sic == SI // 128 - 1)),
                      (("smid", sic), ("wdbf", slw)), ((banks[tl],),))
            for tl in range(4):
                tt = half * 4 + tl
                r("dve", lambda e, tt=tt, oc=oc, bk=banks[tl]: e.tensor_add(
                    sb["ffn_acc"][:, tt * H + oc * 512: tt * H + (oc + 1) * 512],
                    sb["ffn_acc"][:, tt * H + oc * 512: tt * H + (oc + 1) * 512],
                    ps[bk][:, :512]),
                  ((banks[tl],),), (("ffn_acc", tt, oc),))
    B.barrier()
    stSM.close()
    if DBG_STOP_AFTER == "P8":
        _finish(B, nc, sems, sb, dbg_outs, out_d, early=True)
        return nc

    for e_i in range(E):
        for tt in range(NCH):
            csl = tt % 2
            r("pe", lambda e, e_i=e_i, tt=tt: e.transpose(
                ps["b2"][:1, :128],
                sb["combine"][:, tt * E + e_i: tt * E + e_i + 1], sb["identf"][:, :]),
              (("combine", tt), ("identf",)), (("b2",),))
            r("dve", lambda e, csl=csl: e.tensor_copy(
                sb["crow_e"][:1, csl * 128:(csl + 1) * 128], ps["b2"][:1, :128]),
              (("b2",),), (("crow_e", csl),))
            r("pe", lambda e, csl=csl: e.matmul(
                ps["b3"][:, :128], sb["ones1f"][:1, :128],
                sb["crow_e"][:1, csl * 128:(csl + 1) * 128], start=True, stop=True),
              (("crow_e", csl), ("ones1f",)), (("b3",),))
            r("act", lambda e, tt=tt: e.copy(
                sb["crep_e"][:, tt * 128:(tt + 1) * 128], ps["b3"][:, :128]),
              (("b3",),), (("crep_e", tt),))
        crk = tuple(("crep_e", tt) for tt in range(NCH))
        for it in range(6):
            ws = it % 2
            for wname, wdr, wbuf in (("eg", wg_d, "wbf_a"),
                                     ("eu", wu_d, "wbf_b")):
                dma("qp", sb[wbuf][:, ws * 2048:(ws + 1) * 2048].rearrange(
                        "p (c i) -> p c i", i=128),
                    wdr.ap()[e_i, it], ((wbuf, ws),), ((wbuf, ws),))
            for tch in range(2):
                th0 = tch * 512
                bg = "b0" if tch == 0 else "b1"
                bu = "b5" if tch == 0 else "b7"
                xk = tuple(("xlnT", hcp, tt) for hcp in range(16) for tt in range(8))
                for hcp in range(16):
                    r("pe", lambda e, hcp=hcp, th0=th0, bg=bg, ws=ws: e.matmul(
                        ps[bg][:, :512],
                        sb["wbf_a"][:, ws * 2048 + hcp * 128: ws * 2048 + (hcp + 1) * 128],
                        sb["xlnT"][:, hcp * TOK + th0: hcp * TOK + th0 + 512],
                        start=(hcp == 0), stop=(hcp == 15)),
                      (("wbf_a", ws),) + xk, ((bg,),))
                for hcp in range(16):
                    r("pe", lambda e, hcp=hcp, th0=th0, bu=bu, ws=ws: e.matmul(
                        ps[bu][:, :512],
                        sb["wbf_b"][:, ws * 2048 + hcp * 128: ws * 2048 + (hcp + 1) * 128],
                        sb["xlnT"][:, hcp * TOK + th0: hcp * TOK + th0 + 512],
                        start=(hcp == 0), stop=(hcp == 15)),
                      (("wbf_b", ws),) + xk, ((bu,),))
                sgs = tch
                r("act", lambda e, sgs=sgs, bg=bg: e.activation(
                    sb["sgt"][:, sgs * 512:(sgs + 1) * 512], ps[bg][:, :512], AF.Silu),
                  ((bg,),), (("sgt", sgs),))
                r("dve", lambda e, sgs=sgs, bu=bu: e.tensor_mul(
                    sb["sgt"][:, sgs * 512:(sgs + 1) * 512],
                    sb["sgt"][:, sgs * 512:(sgs + 1) * 512], ps[bu][:, :512]),
                  (("sgt", sgs), (bu,)), (("sgt", sgs),))
                r("dve", lambda e, it=it, th0=th0, sgs=sgs: e.tensor_mul(
                    sb["hmid"][:, it * TOK + th0: it * TOK + th0 + 512],
                    sb["sgt"][:, sgs * 512:(sgs + 1) * 512],
                    sb["crep_e"][:, th0:th0 + 512]),
                  (("sgt", sgs),) + crk, (("hmid", it),))
        for oc in range(4):
            for ic in range(6):
                dma("qp", sb["wdbf"][:, ic * 512:(ic + 1) * 512],
                    wd_d.ap()[e_i, ic * 128:(ic + 1) * 128, oc * 512:(oc + 1) * 512],
                    (("wdbf", ic),), (("wdbf", ic),))
            for tt in range(NCH):
                bk = "b2" if tt % 2 == 0 else "b3"
                for ic in range(6):
                    r("pe", lambda e, ic=ic, tt=tt, bk=bk: e.matmul(
                        ps[bk][:, :512],
                        sb["hmid"][:, ic * TOK + tt * 128: ic * TOK + (tt + 1) * 128],
                        sb["wdbf"][:, ic * 512:(ic + 1) * 512],
                        start=(ic == 0), stop=(ic == 5)),
                      (("hmid", ic), ("wdbf", ic)), ((bk,),))
                r("dve", lambda e, tt=tt, oc=oc, bk=bk: e.tensor_add(
                    sb["ffn_acc"][:, tt * H + oc * 512: tt * H + (oc + 1) * 512],
                    sb["ffn_acc"][:, tt * H + oc * 512: tt * H + (oc + 1) * 512],
                    ps[bk][:, :512]),
                  ((bk,),), (("ffn_acc", tt, oc),))

    B.barrier()
    stE.close()
    stDE.close()
    stG = contextlib.ExitStack()
    S(stG, "x2t", [128, H])
    S(stG, "sq2k", [128, H])
    S(stG, "xrt", [128, 2 * H], BF16)
    for tt in range(NCH):
        sl = tt % 2
        dma("qs", sb["xrt"][:, sl * H:(sl + 1) * H],
            x1_sp.ap()[tt * 128:(tt + 1) * 128, :],
            (("xrt", sl), ("x1sp", tt)), (("xrt", sl),))
        fk = tuple(("ffn_acc", tt, oc) for oc in range(4))
        r("dve", lambda e, tt=tt, sl=sl: e.tensor_add(
            sb["x2t"][:, :], sb["ffn_acc"][:, tt * H:(tt + 1) * H],
            sb["xrt"][:, sl * H:(sl + 1) * H]),
          fk + (("xrt", sl),), (("x2t",),))
        r("dve", lambda e: e.tensor_reduce(sb["red1"][:, :], sb["x2t"][:, :],
                                           AX.X, AL.add),
          (("x2t",),), (("red1",),))
        r("dve", lambda e: e.tensor_scalar_mul(sb["red1"][:, :], sb["red1"][:, :],
                                               1.0 / H),
          (("red1",),), (("red1",),))
        r("dve", lambda e: e.tensor_scalar(sb["x2t"][:, :], sb["x2t"][:, :],
                                           sb["red1"][:, :], None, AL.subtract),
          (("x2t",), ("red1",)), (("x2t",),))
        r("dve", lambda e: e.tensor_mul(sb["sq2k"][:, :], sb["x2t"][:, :],
                                        sb["x2t"][:, :]),
          (("x2t",),), (("sq2k",),))
        r("dve", lambda e: e.tensor_reduce(sb["red1b"][:, :], sb["sq2k"][:, :],
                                           AX.X, AL.add),
          (("sq2k",),), (("red1b",),))
        r("act", lambda e: e.activation(sb["red1b"][:, :], sb["red1b"][:, :], AF.Ln,
                                        scale=1.0 / H, bias=1e-5),
          (("red1b",), ("cvals",)), (("red1b",),))
        r("act", lambda e: e.activation(sb["red1b"][:, :], sb["red1b"][:, :],
                                        AF.Exp, scale=-0.5),
          (("red1b",),), (("red1b",),))
        r("dve", lambda e: e.tensor_scalar(sb["x2t"][:, :], sb["x2t"][:, :],
                                           sb["red1b"][:, :], None, AL.mult),
          (("x2t",), ("red1b",)), (("x2t", "f"),))
        dma("qs", out_d.ap()[tt * 128:(tt + 1) * 128, :], sb["x2t"][:, :],
            (("x2t", "f"),), (("outw", tt),))

    _finish(B, nc, sems, sb, dbg_outs, out_d, early=False)
    return nc



def _finish(B, nc, sems, sb, dbg_outs, out_d, early):
    r = B.rec
    for (oname, sname, sl) in DBG_TAPS:
        src = sb[sname][sl[0]:sl[1], sl[2]:sl[3]]
        dr = dbg_outs[oname]
        keys = tuple(k for k in list(B.writers.keys())
                     if isinstance(k, tuple) and len(k) > 0 and k[0] == sname)
        r("qp", lambda e, dr=dr, src=src: e.dma_start(out=dr.ap(), in_=src),
          keys, ((oname,),))
    if early:
        r("qs", lambda e: e.dma_start(out=out_d.ap()[0:128, 0:128],
                                      in_=sb["identf"][:, :]),
          (("identf",),), (("outw", 0),))
    r("pool", lambda e: e.engine_nop(),
      tuple(("outw", tt) for tt in range(NCH)) +
      tuple((o,) for o in dbg_outs), ())
    B.finalize(sems)


def _stage_inputs(inputs):
    x = np.ascontiguousarray(inputs["L0_attn_x"].reshape(8192, H).astype(np.float32))
    xT_full = np.ascontiguousarray(x.T)
    xT_pad = np.concatenate([np.zeros((H, 3), np.float32), xT_full], axis=1)
    wip = np.asarray(inputs["L0_attn_in_proj_w"], np.float32)
    wipT = np.zeros((H, IPD), np.float32)
    wipT[:, :CD + VD] = wip[:CD + VD].T
    wipT[:, CD + VD:CD + VD + 16] = wip[CD + VD:CD + VD + 16].T
    wipT[:, CD + VD + 128:CD + VD + 144] = wip[CD + VD + 16:CD + VD + 32].T
    convw = np.ascontiguousarray(
        np.asarray(inputs["L0_attn_conv_w"], np.float32)
        .reshape(24, 128, KS).transpose(1, 0, 2).reshape(128, 24 * KS))
    wopT = np.ascontiguousarray(np.asarray(inputs["L0_attn_out_proj_w"], np.float32).T)
    wg = np.ascontiguousarray(np.asarray(inputs["L0_ffn_w_gate"], np.float32)
                              .reshape(E, 16, 128, 6, 128).transpose(0, 3, 2, 1, 4))
    wu = np.ascontiguousarray(np.asarray(inputs["L0_ffn_w_up"], np.float32)
                              .reshape(E, 16, 128, 6, 128).transpose(0, 3, 2, 1, 4))
    wsg = np.ascontiguousarray(np.asarray(inputs["L0_ffn_ws_gate"], np.float32)
                               .reshape(16, 128, 32, 128).transpose(2, 1, 0, 3))
    wsu = np.ascontiguousarray(np.asarray(inputs["L0_ffn_ws_up"], np.float32)
                               .reshape(16, 128, 32, 128).transpose(2, 1, 0, 3))
    identf = np.eye(128, dtype=np.float32)
    identb = np.eye(128).astype(BF)
    ones1f = np.ones((1, 128), np.float32)
    ones2b = np.zeros((128, 2), np.float32)
    ones2b[:64, 0] = 1.0
    ones2b[64:, 1] = 1.0
    blk2f = np.zeros((2, 128), np.float32)
    blk2f[0, :64] = 1.0
    blk2f[1, 64:] = 1.0
    tt_, ss_ = np.meshgrid(np.arange(128), np.arange(128), indexing="ij")
    maskS = np.where(ss_ >= tt_, 1e30, 0.0).astype(np.float32)
    maskI = np.where(ss_ > tt_, 1e30, 0.0).astype(np.float32)
    normw = np.ascontiguousarray(
        np.tile(np.asarray(inputs["L0_attn_norm_weight"], np.float32), 16)[None, :])
    shared = {
        "wipT": wipT, "convw": convw,
        "dtb": np.asarray(inputs["L0_attn_dt_bias"], np.float32).reshape(16, 1),
        "alog": np.asarray(inputs["L0_attn_A_log"], np.float32).reshape(16, 1),
        "normw": normw, "wopT": wopT,
        "routerw": np.ascontiguousarray(np.asarray(inputs["L0_ffn_router_w"], np.float32)),
        "wg": wg, "wu": wu,
        "wd": np.ascontiguousarray(np.asarray(inputs["L0_ffn_w_down"], np.float32)),
        "wsg": wsg, "wsu": wsu,
        "wsd": np.ascontiguousarray(np.asarray(inputs["L0_ffn_ws_down"], np.float32)),
        "cvals": np.tile(np.array([[1.0, 1e-6, 1e-5]], np.float32), (128, 1)),
        "identf": identf, "identb": identb, "ones1f": ones1f,
        "ones2b": ones2b.astype(BF), "blk2f": blk2f, "maskS": maskS, "maskI": maskI,
    }
    in_maps = []
    for c in range(NCORE):
        t0 = c * TOK
        m = dict(shared)
        m["xT"] = np.ascontiguousarray(xT_pad[:, t0:t0 + TOKE])
        m["xtm"] = np.ascontiguousarray(x[t0:t0 + TOK])
        segmask = np.zeros((64, NCORE), np.float32)
        segmask[:, :c] = 1.0
        m["segmask"] = segmask
        in_maps.append(m)
    return in_maps


def kernel(**inputs):
    in_maps = _stage_inputs(inputs)
    cores = list(range(NCORE))
    ncA = build_program("A")
    namesA = set(ncA._din)
    mapsA = [{k: v for k, v in m.items() if k in namesA} for m in in_maps]
    resA = run_bass_kernel_spmd(ncA, mapsA, core_ids=cores)
    segall = np.ascontiguousarray(np.concatenate(
        [np.asarray(resA.results[c]["segout"]) for c in range(NCORE)], axis=1))
    ncB = build_program("B")
    namesB = set(ncB._din)
    mapsB = []
    for c in range(NCORE):
        m = {k: v for k, v in in_maps[c].items() if k in namesB}
        ra = resA.results[c]
        m["segall_in"] = segall
        m["pt_in"] = np.asarray(ra["pt_o"])
        m["g_in"] = np.asarray(ra["g_o"])
        m["lo_in"] = np.asarray(ra["lo_o"])
        m["otm_in"] = np.asarray(ra["otm_o"])
        m["tc_in"] = np.asarray(ra["tc_o"])
        m["z_in"] = np.asarray(ra["zspill"])
        mapsB.append(m)
    resB = run_bass_kernel_spmd(ncB, mapsB, core_ids=cores)
    out = np.concatenate([np.asarray(r["out"]) for r in resB.results], axis=0)
    return out.reshape(1, 8192, H).astype(np.float32)
